# version 04 (final)
# speedup vs baseline: 1.1132x; 1.0710x over previous
import os
import numpy as np
import ml_dtypes
from contextlib import ExitStack

import concourse.bass as bass
import concourse.mybir as mybir
from concourse.bass_utils import run_bass_kernel_spmd

F32 = mybir.dt.float32
BF16 = mybir.dt.bfloat16
I32 = mybir.dt.int32
AF = mybir.ActivationFunctionType
ALU = mybir.AluOpType
AX = mybir.AxisListType
DS = bass.ds

B, L, D = 2, 4096, 2048
NCORES, LANES = 8, 4
TPC, CH, NCH, GCH = 1024, 256, 4, 16
D_INNER, D_STATE, NHEADS, HEAD_DIM, D_CONV = 4096, 128, 64, 64, 4
CONV_DIM = D_INNER + 2 * D_STATE
D_IN_PROJ = 2 * D_INNER + 2 * D_STATE + NHEADS
NZC, NXC, NKC = D_INNER // 128, CONV_DIM // 128, D // 128
NMC = 67
I_FFN, E_MOE, I_MOE = 5632, 16, 1024
H_ATT, HKV, HD = 32, 8, 64
LN_EPS = 1e-5



class Prog:

    ENGINES = ("tensor", "vector", "scalar", "gpsimd", "sync")
    AUTO_SEM = {"tensor": "oT", "vector": "oV", "scalar": "oA",
                "gpsimd": "oG", "sync": "oS"}
    SIG = {
        "matmul": ([0], [1, 2]),
        "transpose": ([0], [1, 2]),
        "memset": ([0], []),
        "tensor_copy": ([0], [1]),
        "tensor_tensor": ([0], [1, 2]),
        "tensor_add": ([0], [1, 2]),
        "tensor_sub": ([0], [1, 2]),
        "tensor_mul": ([0], [1, 2]),
        "tensor_max": ([0], [1, 2]),
        "tensor_scalar": ([0], [1, 2, 3]),
        "tensor_single_scalar": ([0], [1, 2]),
        "scalar_tensor_tensor": ([0], [1, 2, 3]),
        "tensor_tensor_scan": ([0], [1, 2, 3]),
        "tensor_reduce": ([0], [1]),
        "reduce_sum": ([0], [1]),
        "reduce_max": ([0], [1]),
        "reciprocal": ([0], [1]),
        "activation": ([0], [1]),
        "select": ([0], [1, 2, 3]),
        "iota": ([0], []),
    }

    def __init__(self, name="prog"):
        self.nc = bass.Bass("TRN2", target_bir_lowering=False)
        self.stack = ExitStack()
        self.alloc_stack = self.stack
        self.thunks = {e: [] for e in self.ENGINES}
        self.cnt = {}
        self.sems = {}
        self.access = {}
        self.last_compute = {}
        self.floor = {e: {} for e in self.ENGINES}
        self.auto = True
        self.prefix = ""
        self.fused = False
        self.inter = set()
        self.dram = {}

    def sbuf(self, name, shape, dt):
        return self.alloc_stack.enter_context(
            self.nc.sbuf_tensor(self.prefix + name, shape, dt))

    def psum(self, name, shape, dt=F32):
        return self.alloc_stack.enter_context(
            self.nc.psum_tensor(self.prefix + name, shape, dt))

    from contextlib import contextmanager

    @contextmanager
    def scope(self):
        sub = ExitStack()
        old = self.alloc_stack
        self.alloc_stack = sub
        try:
            yield
        finally:
            self.alloc_stack = old
            sub.close()

    def sem(self, name):
        if name not in self.sems:
            self.sems[name] = self.stack.enter_context(self.nc.semaphore(name))
            self.cnt[name] = 0
        return self.sems[name]

    def din(self, name, shape, dt):
        if name in self.dram:
            return self.dram[name]
        kind = "Internal" if (self.fused and name in self.inter) else "ExternalInput"
        if kind == "Internal":
            ap = self.nc.dram_tensor(name, shape, dt).ap()
        else:
            ap = self.nc.dram_tensor(name, shape, dt, kind=kind).ap()
        self.dram[name] = ap
        return ap

    def dout(self, name, shape, dt):
        if name in self.dram:
            return self.dram[name]
        if self.fused and name in self.inter:
            ap = self.nc.dram_tensor(name, shape, dt).ap()
        else:
            ap = self.nc.dram_tensor(name, shape, dt, kind="ExternalOutput").ap()
        self.dram[name] = ap
        return ap

    def dint(self, name, shape, dt):
        return self.nc.dram_tensor(self.prefix + name, shape, dt).ap()

    @staticmethod
    def _span(ap):
        t = ap.tensor
        name = t.name
        off = ap.offset
        hi = off
        width = 1
        dims = list(ap.ap)
        for i, (step, count) in enumerate(dims):
            if count > 1:
                hi += abs(step) * (count - 1)
            if i > 0:
                width *= count
        space = "PS" if "PSum" in type(t).__name__ else (
            "SB" if "SB" in type(t).__name__ else "DR")
        return [name, off, hi, max(width, 1), space]

    def _collect(self, method, args, kw):
        wi, ri = self.SIG.get(method, (None, None))
        writes, reads = [], []
        if method == "dma_start":
            if "out" in kw and isinstance(kw["out"], bass.AP):
                writes.append(self._span(kw["out"]))
            if "in_" in kw and isinstance(kw["in_"], bass.AP):
                reads.append(self._span(kw["in_"]))
            return writes, reads
        if wi is None:
            return None, None
        for i in wi:
            if i < len(args) and isinstance(args[i], bass.AP):
                writes.append(self._span(args[i]))
        for i in ri:
            if i < len(args) and isinstance(args[i], bass.AP):
                reads.append(self._span(args[i]))
        for k in ("bias", "scale"):
            if isinstance(kw.get(k), bass.AP):
                reads.append(self._span(kw[k]))
        return writes, reads

    @staticmethod
    def _overlap(a, b):
        if a[0] != b[0]:
            return False
        if a[4] == "PS":
            return True
        return not (a[2] < b[1] or b[2] < a[1])

    def _need_edge(self, rec, span_b, engine_b, width_b, is_write_b):
        if rec["engine"] == engine_b:
            if rec["engine"] == "sync" or rec["kind"] == "dma":
                pass
            elif rec["width"] >= 64 and width_b >= 64:
                return None
        return self._completion(rec)

    def _completion(self, rec):
        if rec["inc"] is not None:
            return (rec["inc"][0], rec["inc"][1])
        if rec["kind"] == "dma":
            return None
        if rec["auto_inc"] is not None:
            return rec["auto_inc"]
        eng = rec["engine"]
        last = self.last_compute.get(eng)
        if last is None:
            return None
        if last["auto_inc"] is None and last["inc"] is None:
            s = self.AUTO_SEM[eng]
            self.sem(s)
            self.cnt[s] += 1
            last["auto_inc"] = (s, self.cnt[s])
        return last["auto_inc"] if last["auto_inc"] is not None else last["inc"]

    def _auto_edges(self, engine, writes, reads, wait_hs, is_dma_consumer=False):
        need = {}

        def add(e):
            if e is None:
                return
            s, v = e
            if need.get(s, 0) < v:
                need[s] = v

        def skip(rec, sp):
            return (not is_dma_consumer and rec["engine"] == engine
                    and rec["kind"] != "dma"
                    and rec["width"] >= 64 and sp[3] >= 64)
        for sp in reads:
            for rec in self.access.get(sp[0], []):
                if rec["is_write"] and self._overlap(rec["span"], sp):
                    if skip(rec, sp):
                        continue
                    add(self._completion(rec))
        for sp in writes:
            for rec in self.access.get(sp[0], []):
                if self._overlap(rec["span"], sp):
                    if skip(rec, sp):
                        continue
                    add(self._completion(rec))
        out = []
        have = dict(self.floor[engine])
        for h, v, s in wait_hs:
            have[s] = max(have.get(s, 0), v)
        for s, v in need.items():
            if have.get(s, 0) >= v:
                continue
            out.append((s, v))
        return out

    def _record(self, engine, method, writes, reads, inc_h, kind="compute"):
        rec = {"engine": engine, "kind": kind,
               "inc": inc_h, "auto_inc": None,
               "is_write": True, "span": None, "width": 1}
        for sp in writes:
            r = dict(rec)
            r["span"] = sp
            r["width"] = sp[3]
            self.access.setdefault(sp[0], []).append(r)
            if kind == "compute":
                self.last_compute[engine] = r
        for sp in reads:
            r = dict(rec)
            r["is_write"] = False
            r["span"] = sp
            r["width"] = sp[3]
            self.access.setdefault(sp[0], []).append(r)
            if kind == "compute":
                self.last_compute[engine] = r
        if not writes and not reads and kind == "compute":
            r = dict(rec)
            r["span"] = ["__none__", 0, 0, 1, "SB"]
            self.last_compute[engine] = r
        return rec

    def op(self, engine, method, *args, wait=None, inc=None, incn=1, **kw):
        waits = [] if wait is None else ([wait] if isinstance(wait, tuple) else list(wait))
        fl = self.floor[engine]
        wait_hs = []
        for s, v in waits:
            if fl.get(s, -1) >= v:
                continue
            self.sem(s)
            wait_hs.append((self.sems[s], v, s))
        inc_h = None
        inc_rec = None
        if inc is not None:
            self.sem(inc)
            self.cnt[inc] += incn
            inc_h = (inc, self.cnt[inc])
        is_dma = method in ("dma_start", "dma_start_transpose")
        writes, reads = self._collect(method, args, kw)
        auto = []
        if self.auto and writes is not None:
            auto = self._auto_edges(engine, writes, reads, wait_hs,
                                    is_dma_consumer=is_dma)
        for s, v in auto:
            self.sem(s)
            wait_hs.append((self.sems[s], v, s))
        for h, v, s in wait_hs:
            if fl.get(s, -1) < v:
                fl[s] = v
        if writes is not None:
            self._record(engine, method, writes, reads, inc_h,
                         "dma" if is_dma else "compute")
        lc = self.last_compute.get(engine) if (writes is not None and not is_dma) else None

        def thunk(eng, wait_hs=wait_hs, inc_h=inc_h, lc=lc, incn=incn):
            for h, v, _s in wait_hs:
                eng.wait_ge(h, v)
            ins = getattr(eng, method)(*args, **kw)
            if inc_h is not None:
                ins.then_inc(self.sems[inc_h[0]], incn)
            if lc is not None and lc.get("auto_inc") is not None \
                    and not lc.get("_emitted", False):
                ins.then_inc(self.sems[lc["auto_inc"][0]], 1)
                lc["_emitted"] = True
        self.thunks[engine].append(thunk)
        return self.cnt[inc] if inc is not None else None

    def wait(self, engine, sem, value):
        if self.floor[engine].get(sem, -1) >= value:
            return
        self.floor[engine][sem] = value
        self.sem(sem)
        h = self.sems[sem]
        self.thunks[engine].append(lambda eng: eng.wait_ge(h, value))

    def raw(self, engine, fn, writes=None, reads=None):
        for ap in (writes or []):
            sp = self._span(ap)
            self.access.setdefault(sp[0], []).append(
                {"engine": engine, "kind": "dma", "inc": None, "auto_inc": None,
                 "is_write": True, "span": sp, "width": sp[3]})
        for ap in (reads or []):
            sp = self._span(ap)
            self.access.setdefault(sp[0], []).append(
                {"engine": engine, "kind": "dma", "inc": None, "auto_inc": None,
                 "is_write": False, "span": sp, "width": sp[3]})
        self.thunks[engine].append(fn)

    def c(self, sem):
        self.sem(sem)
        return self.cnt[sem]

    def bump(self, sem, n=1):
        self.sem(sem)
        self.cnt[sem] += n
        return self.cnt[sem]

    def barrier(self, engines=None):
        engines = engines or self.ENGINES
        for e in self.ENGINES:
            lc = self.last_compute.get(e)
            if lc is not None and lc["inc"] is None and lc["auto_inc"] is None:
                s = self.AUTO_SEM[e]
                self.sem(s)
                self.cnt[s] += 1
                lc["auto_inc"] = (s, self.cnt[s])
        vals = [(s, self.cnt[s]) for s in list(self.cnt)]
        for e in engines:
            for s, v in vals:
                if v > 0:
                    self.wait(e, s, v)
        self.access = {}

    def finalize(self):
        block = self.stack.enter_context(self.nc.Block())
        for ename in self.ENGINES:
            ths = self.thunks[ename]
            if not ths:
                continue

            def runner(eng, ths=ths):
                for t in ths:
                    t(eng)
            getattr(block, ename)(runner)
        self.stack.close()
        return self.nc


class PsumPool:

    def __init__(self, prog, nbanks=8, prefix="pb"):
        self.prog = prog
        self.banks = [prog.psum(f"{prefix}{i}", [128, 512], F32) for i in range(nbanks)]
        self.guard = [None] * nbanks
        self.next = 0
        self.n = nbanks

    def take(self, rows=128, cols=512, hold=False):
        for _ in range(self.n):
            b = self.next
            self.next = (self.next + 1) % self.n
            if self.guard[b] != "held":
                break
        else:
            raise RuntimeError("all psum banks held")
        g = self.guard[b]
        self.guard[b] = "held" if hold else None
        return b, self.banks[b][0:rows, 0:cols], ([g] if (g and g != "held") else [])

    def release(self, b, sem, value):
        self.guard[b] = (sem, value)



def rope_perm_hd():
    base = np.arange(HD)
    return np.concatenate([base[0::2], base[1::2]])


def core_token_rows(core):
    lane = core % LANES
    return [(LANES * m + lane) for m in range(NCH)]


def lhsT_blocks(w, part=128, width=128):
    Kd, O = w.shape
    nk = Kd // part
    nmb = (O + width - 1) // width
    out = np.zeros((nmb, part, nk, width), np.float32)
    wr = w.reshape(nk, part, O)
    for mb in range(nmb):
        wslice = wr[:, :, mb * width:(mb + 1) * width]
        out[mb, :, :, 0:wslice.shape[2]] = wslice.transpose(1, 0, 2)
    return np.ascontiguousarray(out)


def host_prepare(inputs):
    x = np.asarray(inputs["L0_attn_hidden_states"])
    shared = {}
    shared["w_in_b"] = lhsT_blocks(np.asarray(inputs["L0_attn_W_in"], np.float32))
    cw = np.asarray(inputs["L0_attn_conv_w"], np.float32).reshape(NXC, 128, D_CONV)
    shared["conv_w_t"] = np.ascontiguousarray(cw.transpose(1, 0, 2).reshape(128, NXC * D_CONV))
    shared["conv_b_t"] = np.ascontiguousarray(
        np.asarray(inputs["L0_attn_conv_b"], np.float32).reshape(NXC, 128).T)
    shared["dt_bias_c"] = np.asarray(inputs["L0_attn_dt_bias"], np.float32).reshape(NHEADS, 1)
    shared["ident_f32"] = np.eye(128, dtype=np.float32)
    per_core = []
    for c in range(NCORES):
        b, lane = c // LANES, c % LANES
        rows = []
        for g in core_token_rows(c):
            rows.append(x[b, CH * g:CH * (g + 1), :])
        x_tm = np.ascontiguousarray(np.concatenate(rows, 0).astype(np.float32))
        hm = np.zeros((128, NCH), np.float32) + 1.0
        if lane == 0:
            hm[:, 0] = 0.0
        meta = np.zeros((1, 16), np.int32)
        for m in range(NCH):
            if lane > 0:
                r, mp = lane - 1, m
            else:
                r, mp = 3, max(m - 1, 0)
            meta[0, 2 * m] = r * CONV_DIM
            meta[0, 2 * m + 1] = 3 * mp
        meta_b = np.zeros((1, 8), np.int32)
        for m in range(NCH):
            meta_b[0, m] = (LANES * m + lane) * 128
        gup = np.asarray(inputs["L0_ffn_gate_up_packed"])
        gus = np.asarray(inputs["L0_ffn_gate_up_scales"])
        dnp = np.asarray(inputs["L0_ffn_down_packed"])
        dns = np.asarray(inputs["L0_ffn_down_scales"])
        gup_s = np.ascontiguousarray(gup[1408 * c:1408 * (c + 1), :])
        gusc = np.ascontiguousarray(
            gus[:, 1408 * c:1408 * (c + 1)].T.reshape(11, 128, 16)
            .transpose(1, 0, 2).reshape(128, 176))
        dnp_s = np.ascontiguousarray(dnp[256 * c:256 * (c + 1), :])
        dnsc = np.ascontiguousarray(
            dns[:, 256 * c:256 * (c + 1)].T.reshape(2, 128, 44)
            .transpose(1, 0, 2).reshape(128, 88))
        d = dict(x_tm=x_tm, halo_mask=hm, meta_a=meta, meta_b=meta_b,
                 gup_s=gup_s, gusc=gusc, dnp_s=dnp_s, dnsc=dnsc)
        per_core.append(d)
    shared["A_col"] = np.asarray(inputs["L0_attn_A"], np.float32).reshape(NHEADS, 1)
    shared["D_col"] = np.asarray(inputs["L0_attn_D"], np.float32).reshape(NHEADS, 1)
    shared["norm_w_t"] = np.ascontiguousarray(
        np.asarray(inputs["L0_attn_norm_w"], np.float32).reshape(NZC, 128).T)
    shared["w_out_b"] = lhsT_blocks(np.asarray(inputs["L0_attn_W_out"], np.float32))
    shared["ident_bf"] = np.eye(128, dtype=ml_dtypes.bfloat16)
    l0 = np.arange(256)[None, :] >= np.arange(128)[:, None]
    shared["tri0"] = l0.astype(np.float32)
    l1 = np.arange(128, 256)[None, :] >= (128 + np.arange(128))[:, None]
    shared["tri1"] = l1.astype(np.float32)
    Dv = np.asarray(inputs["L0_attn_D"], np.float32)
    dfm = np.zeros((128, NZC), np.float32)
    for fc in range(NZC):
        dfm[0:64, fc] = Dv[2 * fc]
        dfm[64:128, fc] = Dv[2 * fc + 1]
    shared["D_fm"] = dfm
    hr = np.zeros((64, 4096), ml_dtypes.bfloat16)
    for k in range(64):
        hr[k, 64 * k:64 * (k + 1)] = 1.0
    shared["hrep"] = hr
    HD2 = HD // 2
    rope_perm = np.concatenate([np.arange(0, HD, 2), np.arange(1, HD, 2)])
    qorder = []
    for g4 in range(4):
        for i in range(4):
            qorder += [8 * g4 + i, 8 * g4 + i + 4]
    shared["qorder"] = np.array(qorder)
    wq = np.asarray(inputs["L1_attn_wq"], np.float32)
    wk = np.asarray(inputs["L1_attn_wk"], np.float32)
    wv = np.asarray(inputs["L1_attn_wv"], np.float32)
    wo = np.asarray(inputs["L1_attn_wo"], np.float32)
    qt = wq.T.reshape(D, H_ATT, HD)[:, qorder, :][:, :, rope_perm]
    shared["wq_b"] = lhsT_blocks(qt.reshape(D, D))
    kt = wk.T.reshape(D, HKV, HD)[:, :, rope_perm]
    shared["wk_b"] = lhsT_blocks(kt.reshape(D, HKV * HD))
    shared["wv_b"] = lhsT_blocks(wv.T)
    wo_p = wo.T.reshape(H_ATT, HD, D)[qorder, :, :].reshape(D, D)
    shared["wo_b"] = lhsT_blocks(wo_p)
    theta = 1.0 / 10000.0 ** (np.arange(0, HD, 2, dtype=np.float32) / HD)
    cdm0 = (np.arange(256)[None, :] >= np.arange(128)[:, None]).astype(np.float32)
    cdm1 = (np.arange(256)[None, :] >= (128 + np.arange(128))[:, None]).astype(np.float32)
    shared["cdm0"] = cdm0
    shared["cdm1"] = cdm1
    for c in range(NCORES):
        lane = c % LANES
        tg = np.concatenate([np.arange(CH * (LANES * m + lane),
                                       CH * (LANES * m + lane) + CH)
                             for m in range(NCH)]).astype(np.float32)
        fr = tg[None, :] * theta[(np.arange(128) % 32)][:, None]
        cosq = np.cos(fr).astype(np.float32)
        sinq = np.sin(fr).astype(np.float32)
        csb = sinq.copy()
        half = (np.arange(128) % 64) < 32
        csb[half, :] *= -1.0
        per_core[c]["cos_t"] = np.ascontiguousarray(cosq)
        per_core[c]["csb_t"] = np.ascontiguousarray(csb)
        mk = np.zeros((128, NCH * 8 * 3), np.float32)
        for m in range(NCH):
            g2 = 2 * (LANES * m + lane)
            for f in range(8):
                jk = 8 * m + f
                base = (m * 8 + f) * 3
                if jk < g2:
                    mk[:, base] = 1.0
                elif jk == g2:
                    mk[:, base + 1] = 1.0
                elif jk == g2 + 1:
                    mk[:, base + 2] = 1.0
        per_core[c]["mksel"] = mk
    shared["wr_t"] = np.ascontiguousarray(np.asarray(inputs["L1_ffn_Wr"], np.float32).T)
    shared["bias_bc"] = np.ascontiguousarray(
        np.broadcast_to(np.asarray(inputs["L1_ffn_bias"], np.float32)[None, :],
                        (128, E_MOE)).copy())
    shared["moe_wg"] = np.stack([lhsT_blocks(np.asarray(inputs["L1_ffn_Wg"][e], np.float32))
                                 for e in range(E_MOE)])
    shared["moe_wu"] = np.stack([lhsT_blocks(np.asarray(inputs["L1_ffn_Wu"][e], np.float32))
                                 for e in range(E_MOE)])
    shared["moe_wd"] = np.stack([lhsT_blocks(np.asarray(inputs["L1_ffn_Wd"][e], np.float32))
                                 for e in range(E_MOE)])
    shared["swg"] = lhsT_blocks(np.asarray(inputs["L1_ffn_sWg"], np.float32))
    shared["swu"] = lhsT_blocks(np.asarray(inputs["L1_ffn_sWu"], np.float32))
    shared["swd"] = lhsT_blocks(np.asarray(inputs["L1_ffn_sWd"], np.float32))
    return shared, per_core



def build_A(P=None, pp=None, pb16=None, debug=False):
    if P is None:
        P = Prog("A")
    P.prefix = "A_"

    x_tm = P.din("x_tm", [TPC, D], F32)
    w_in = P.din("w_in_b", [NMC, 128, NKC, 128], F32)
    conv_w = P.din("conv_w_t", [128, NXC * D_CONV], F32)
    conv_b = P.din("conv_b_t", [128, NXC], F32)
    dt_bias = P.din("dt_bias_c", [NHEADS, 1], F32)
    halo_mask = P.din("halo_mask", [128, NCH], F32)
    meta_a = P.din("meta_a", [1, 16], I32)
    ident = P.din("ident_f32", [128, 128], F32)

    o_xfm = P.dout("x_fm", [D, TPC], F32)
    o_xn = P.dout("xn", [D, TPC], BF16)
    o_z = P.dout("z", [D_INNER, TPC], BF16)
    o_xbc = P.dout("xbc", [CONV_DIM, TPC], BF16)
    o_dt = P.dout("dt", [NHEADS, TPC], F32)

    o_dbg = P.dout("dbg", [128, 512 + 64], F32) if debug else None
    ag_in = P.dint("halo_in", [CONV_DIM, NCH * 3], BF16)
    ag_out = P.dint("halo_out", [LANES * CONV_DIM, NCH * 3], BF16)

    s_xtm = [P.sbuf(f"s_xtm{i}", [128, D], F32) for i in range(2)]
    s_stage = [P.sbuf(f"s_stage{i}", [128, NKC * 128], F32) for i in range(2)]
    s_xn = P.sbuf("s_xn", [128, NKC * TPC], BF16)
    s_xbcp = P.sbuf("s_xbcp", [128, NXC * TPC], BF16)
    s_sq = P.sbuf("s_sq", [128, D], F32)
    s_stats = P.sbuf("s_stats", [128, 8 * 4], F32)
    s_diag = P.sbuf("s_diag", [128, 256], F32)
    s_abbc = [P.sbuf(f"s_abbc{i}", [128, 256], F32) for i in range(2)]
    s_ones = P.sbuf("s_ones", [128, 128], F32)
    s_ident = P.sbuf("s_ident", [128, 128], F32)
    s_eps = P.sbuf("s_eps", [128, 1], F32)
    s_one = P.sbuf("s_one", [128, 1], F32)
    s_w = [P.sbuf(f"s_w{i}", [128, NKC * 128], BF16) for i in range(2)]
    s_zst = [P.sbuf(f"s_zst{i}", [128, 512], BF16) for i in range(4)]
    s_dtr = P.sbuf("s_dtr", [64, TPC], F32)
    s_dts = P.sbuf("s_dts", [64, 3 * TPC], F32)
    s_dt = P.sbuf("s_dt", [64, TPC], F32)
    s_cw = P.sbuf("s_cw", [128, NXC * D_CONV], F32)
    s_cb = P.sbuf("s_cb", [128, NXC], F32)
    s_dtb = P.sbuf("s_dtb", [NHEADS, 1], F32)
    s_hm = P.sbuf("s_hm", [128, NCH], F32)
    s_meta = P.sbuf("s_meta", [1, 16], I32)
    s_halo = P.sbuf("s_halo", [128, NXC * NCH * 3], BF16)
    s_ext = P.sbuf("s_ext", [128, CH + 3], F32)
    s_dbg_buf = P.sbuf("s_dbg_buf", [128, 320], F32) if debug else None
    s_acc = P.sbuf("s_acc", [128, CH], F32)

    if pp is None:
        pp = PsumPool(P, nbanks=6)

    P.op("sync", "dma_start", out=s_ident[:], in_=ident[:], inc="cl", incn=16)
    P.op("sync", "dma_start", out=s_cw[:], in_=conv_w[:], inc="cl", incn=16)
    P.op("sync", "dma_start", out=s_cb[:], in_=conv_b[:], inc="cl", incn=16)
    P.op("sync", "dma_start", out=s_dtb[:], in_=dt_bias[:], inc="cl", incn=16)
    P.op("sync", "dma_start", out=s_hm[:], in_=halo_mask[:], inc="cl", incn=16)
    P.op("sync", "dma_start", out=s_meta[:], in_=meta_a[:], inc="cl", incn=16)
    cl_all = P.c("cl")
    P.op("vector", "memset", s_ones[:], 1.0)
    P.op("vector", "memset", s_one[:], 1.0)
    P.op("vector", "memset", s_eps[:], LN_EPS, inc="ve")

    xtm_guard = [[], []]
    stage_guard = [None, None]
    abbc_guard = [None, None]
    diag_guard = []

    for j in range(8):
        bi = j % 2
        w = [("cl", cl_all)] + xtm_guard[bi]
        P.op("sync", "dma_start", out=s_xtm[bi][:], in_=x_tm[j * 128:(j + 1) * 128, :],
             wait=w, inc="xl", incn=16)
        xl_j = P.c("xl")
        P.op("scalar", "activation", s_sq[:], s_xtm[bi][:], AF.Square,
             wait=[("xl", xl_j), ("ve", P.c("ve"))], inc="act")
        c0 = s_stats[:, 4 * j:4 * j + 1]
        c1 = s_stats[:, 4 * j + 1:4 * j + 2]
        c2 = s_stats[:, 4 * j + 2:4 * j + 3]
        c3 = s_stats[:, 4 * j + 3:4 * j + 4]
        P.op("vector", "reduce_sum", c0, s_xtm[bi][:], AX.X,
             wait=("xl", xl_j), inc="ve")
        P.op("vector", "reduce_sum", c1, s_sq[:], AX.X,
             wait=("act", P.c("act")), inc="ve")
        P.op("vector", "tensor_scalar", c3, c0, 1.0 / D, None, ALU.mult,
             wait=("ve", P.c("ve")), inc="ve")
        P.op("vector", "tensor_scalar", c2, c1, 1.0 / D, None, ALU.mult, inc="ve")
        P.op("vector", "tensor_mul", s_sq[:, 0:1], c3, c3,
             wait=("ve", P.c("ve")), inc="ve")
        P.op("vector", "tensor_sub", c2, c2, s_sq[:, 0:1],
             wait=("ve", P.c("ve")), inc="ve")
        P.op("scalar", "activation", c2, c2, AF.Ln, bias=s_eps[:, 0:1],
             wait=("ve", P.c("ve")))
        P.op("scalar", "activation", c2, c2, AF.Exp, scale=-0.5, inc="act")
        P.op("vector", "scalar_tensor_tensor", c3, c3, -1.0, c2, ALU.mult, ALU.mult,
             wait=("act", P.c("act")), inc="ve")
        P.wait("vector", "ve", P.c("ve"))
        dg = [abbc_guard[bi]] if abbc_guard[bi] else []
        P.op("vector", "tensor_scalar", s_diag[:, 0:128], s_ident[:], c2, None, ALU.mult,
             wait=(diag_guard if diag_guard else None))
        P.op("vector", "tensor_scalar", s_diag[:, 128:256], s_ident[:], c3, None,
             ALU.mult, inc="ve")
        ve_diag = P.c("ve")
        bnk, pap, g = pp.take(128, 256)
        P.op("tensor", "matmul", pap[:, 0:128], s_ones[:], s_diag[:, 0:128],
             start=True, stop=True, wait=[("ve", ve_diag)] + g)
        P.op("tensor", "matmul", pap[:, 128:256], s_ones[:], s_diag[:, 128:256],
             start=True, stop=True, inc="pe")
        diag_guard = [("pe", P.c("pe"))]
        P.op("vector", "tensor_copy", s_abbc[bi][:], pap[:],
             wait=[("pe", P.c("pe"))] + dg, inc="ve")
        pp.release(bnk, "ve", P.c("ve"))
        ve_abbc = P.c("ve")

        sg = [stage_guard[bi]] if stage_guard[bi] else []
        for grp in range(4):
            bnk, pap, g = pp.take(128, 512)
            for q in range(4):
                dk = grp * 4 + q
                kw = {}
                w = ([("xl", xl_j)] + g) if q == 0 else None
                P.op("tensor", "transpose", pap[:, q * 128:(q + 1) * 128],
                     s_xtm[bi][:, dk * 128:(dk + 1) * 128], s_ident[:],
                     wait=w, inc=("pe" if q == 3 else None))
            P.op("vector", "tensor_copy",
                 s_stage[bi][:, grp * 512:(grp + 1) * 512], pap[:],
                 wait=[("pe", P.c("pe"))] + (sg if grp == 0 else []), inc="ve")
            pp.release(bnk, "ve", P.c("ve"))
        xtm_guard[bi] = [("pe", P.c("pe")), ("ve", P.c("ve"))]
        ve_stage = P.c("ve")

        for dk in range(NKC):
            st_sl = s_stage[bi][:, dk * 128:(dk + 1) * 128]
            P.op("vector", "tensor_mul", s_sq[:, 0:128], st_sl, s_abbc[bi][:, 0:128],
                 wait=("ve", ve_stage) if dk == 0 else None)
            P.op("vector", "tensor_add", s_xn[:, dk * TPC + j * 128: dk * TPC + (j + 1) * 128],
                 s_sq[:, 0:128], s_abbc[bi][:, 128:256], inc="ve")
        abbc_guard[bi] = ("ve", P.c("ve"))

        xfm_dst = o_xfm[:, j * 128:(j + 1) * 128].rearrange("(k p) t -> p k t", p=128)
        xfm_src = s_stage[bi][:].rearrange("p (k t) -> p k t", t=128)
        P.op("scalar", "dma_start", out=xfm_dst, in_=xfm_src,
             wait=("ve", ve_stage), inc="st", incn=16)
        stage_guard[bi] = ("st", P.c("st"))
    ve_xn_done = P.c("ve")
    if debug:
        P.op("vector", "tensor_copy", s_dbg_buf[:, 0:32], s_stats[:],
             wait=("ve", ve_xn_done))
        P.op("vector", "tensor_copy", s_dbg_buf[:, 64:64 + 256], s_abbc[0][:], inc="ve")
        P.op("scalar", "dma_start", out=o_dbg[:, 0:320], in_=s_dbg_buf[:, 0:320],
             wait=("ve", P.c("ve")), inc="st", incn=16)
    xn_dst = o_xn.rearrange("(k p) t -> p k t", p=128)
    P.op("scalar", "dma_start", out=xn_dst,
         in_=s_xn[:].rearrange("p (k t) -> p k t", t=TPC),
         wait=("ve", ve_xn_done), inc="st", incn=16)

    pe_mblock = {}
    zst_guard = [None] * 4
    for m in range(NMC):
        W = 128 if m < 66 else 64
        bi = m % 2
        src = w_in[m, :, :, 0:W]
        dst = s_w[bi][:, 0:NKC * W].rearrange("p (k o) -> p k o", o=W)
        w = [("pe", pe_mblock[m - 2])] if m >= 2 else None
        P.op("gpsimd", "dma_start", out=dst, in_=src, wait=w, inc="wl", incn=16)
        wl_m = P.c("wl")
        for tt in range(2):
            bnk, pap, g = pp.take(W, 512)
            for k in range(NKC):
                w = []
                if k == 0:
                    w = [("wl", wl_m), ("ve", ve_xn_done)] + g
                P.op("tensor", "matmul", pap,
                     s_w[bi][:, k * W:(k + 1) * W],
                     s_xn[:, k * TPC + tt * 512: k * TPC + tt * 512 + 512],
                     start=(k == 0), stop=(k == NKC - 1),
                     wait=(w if w else None), inc=("pe" if k == NKC - 1 else None))
            pe_here = P.c("pe")
            if m < 32:
                zi = (2 * m + tt) % 4
                sb = s_zst[zi]
                zw = [("pe", pe_here)] + ([zst_guard[zi]] if zst_guard[zi] else [])
                P.op("vector", "tensor_copy", sb[:], pap, wait=zw, inc="ve")
                P.op("scalar", "dma_start",
                     out=o_z[128 * m:128 * (m + 1), tt * 512:tt * 512 + 512],
                     in_=sb[:], wait=("ve", P.c("ve")), inc="st", incn=16)
                zst_guard[zi] = ("st", P.c("st"))
            elif m < 66:
                xc = m - 32
                P.op("vector", "tensor_copy",
                     s_xbcp[:, xc * TPC + tt * 512: xc * TPC + tt * 512 + 512], pap,
                     wait=("pe", pe_here), inc="ve")
            else:
                P.op("vector", "tensor_copy", s_dtr[:, tt * 512:tt * 512 + 512],
                     pap[0:64, :], wait=("pe", pe_here), inc="ve")
            pp.release(bnk, "ve", P.c("ve"))
        pe_mblock[m] = P.c("pe")
    ve_xbcp_done = P.c("ve")

    agin_dst = ag_in.rearrange("(c p) t -> p c t", p=128) \
        .rearrange("p c (m u) -> p c m u", u=3)
    agin_src = s_xbcp[:].rearrange("p (c t) -> p c t", t=TPC) \
        .rearrange("p c (m u) -> p c m u", u=CH)
    for m in range(NCH):
        P.op("sync", "dma_start",
             out=agin_dst[:, :, m, :],
             in_=agin_src[:, :, m, 253:256],
             wait=("ve", ve_xbcp_done), inc="st", incn=16)
    st_agin = P.c("st")
    P.wait("gpsimd", "st", st_agin)

    def cc_thunk(eng):
        eng.collective_compute(
            "AllGather", ALU.bypass,
            replica_groups=[[0, 1, 2, 3], [4, 5, 6, 7]],
            ins=[ag_in.opt()], outs=[ag_out.opt()],
        ).then_inc(P.sems["cc"], 1)
    P.sem("cc")
    P.raw("gpsimd", cc_thunk)
    P.bump("cc", 1)
    cc_done = P.c("cc")

    def halo_load(eng, m=0):
        eng.wait_ge(P.sems["cl"], cl_all)
        eng.wait_ge(P.sems["cc"], cc_done)
        rb = eng.value_load(s_meta[0:1, 2 * m:2 * m + 1])
        cb = eng.value_load(s_meta[0:1, 2 * m + 1:2 * m + 2])
        src = ag_out[DS(rb, CONV_DIM), :].rearrange("(c p) t -> p c t", p=128)[:, :, DS(cb, 3)]
        dst = s_halo[:].rearrange("p (c m u) -> p c m u", m=NCH, u=3)[:, :, m, :]
        eng.dma_start(out=dst, in_=src).then_inc(P.sems["hl"], 16)
    P.sem("hl")
    for m in range(NCH):
        P.raw("gpsimd", lambda eng, m=m: halo_load(eng, m))
        P.bump("hl", 16)
    hl_done = P.c("hl")

    for m in range(NCH):
        for c in range(NXC):
            base = c * TPC + m * CH
            hsl = s_halo[:].rearrange("p (cc mm u) -> p cc mm u", mm=NCH, u=3)[:, c, m, :]
            P.op("vector", "tensor_scalar", s_ext[:, 0:3], hsl, s_hm[:, m:m + 1],
                 None, ALU.mult, wait=[("hl", hl_done), ("act", P.c("act"))],
                 inc="ve")
            P.op("vector", "tensor_copy", s_ext[:, 3:CH + 3],
                 s_xbcp[:, base:base + CH])
            P.op("vector", "tensor_scalar", s_acc[:], s_ext[:, 0:CH],
                 s_cw[:, 4 * c:4 * c + 1], s_cb[:, c:c + 1], ALU.mult, ALU.add,
                 wait=("ve", P.c("ve")))
            for k in range(1, 4):
                P.op("vector", "scalar_tensor_tensor", s_acc[:],
                     s_ext[:, k:k + CH], s_cw[:, 4 * c + k:4 * c + k + 1], s_acc[:],
                     ALU.mult, ALU.add, inc=("ve" if k == 3 else None))
            P.op("scalar", "activation", s_xbcp[:, base:base + CH], s_acc[:],
                 AF.Silu, wait=("ve", P.c("ve")), inc="act")
    act_conv_done = P.c("act")
    xbc_dst = o_xbc.rearrange("(c p) t -> p c t", p=128)
    P.op("scalar", "dma_start", out=xbc_dst,
         in_=s_xbcp[:].rearrange("p (c t) -> p c t", t=TPC),
         wait=("act", act_conv_done), inc="st", incn=16)

    P.op("vector", "tensor_scalar", s_dts[:, 0:TPC], s_dtr[:], s_dtb[:, 0:1], None,
         ALU.add, wait=("ve", ve_xbcp_done), inc="ve")
    P.op("scalar", "activation", s_dts[:, TPC:2 * TPC], s_dts[:, 0:TPC], AF.Abs,
         wait=("ve", P.c("ve")))
    P.op("scalar", "activation", s_dts[:, TPC:2 * TPC], s_dts[:, TPC:2 * TPC],
         AF.Exp, scale=-1.0)
    P.op("scalar", "activation", s_dts[:, TPC:2 * TPC], s_dts[:, TPC:2 * TPC],
         AF.Ln, bias=s_one[0:64, 0:1])
    P.op("scalar", "activation", s_dts[:, 2 * TPC:3 * TPC], s_dts[:, 0:TPC],
         AF.Relu, inc="act")
    P.op("vector", "tensor_add", s_dt[:], s_dts[:, TPC:2 * TPC],
         s_dts[:, 2 * TPC:3 * TPC], wait=("act", P.c("act")), inc="ve")
    P.op("scalar", "dma_start", out=o_dt[:], in_=s_dt[:],
         wait=("ve", P.c("ve")), inc="st", incn=16)

    st_all = P.c("st")
    P.wait("gpsimd", "st", st_all)
    return P


def _unused():
    pass





def build_B(P=None, pp=None, pb16=None):
    if P is None:
        P = Prog("B")
    P.prefix = "B_"

    i_xbc = P.din("xbc", [CONV_DIM, TPC], BF16)
    i_dt = P.din("dt", [NHEADS, TPC], F32)
    i_z = P.din("z", [D_INNER, TPC], BF16)
    A_col = P.din("A_col", [NHEADS, 1], F32)
    D_fm = P.din("D_fm", [128, NZC], F32)
    norm_w = P.din("norm_w_t", [128, NZC], F32)
    w_out = P.din("w_out_b", [16, 128, NZC, 128], F32)
    ident_f = P.din("ident_f32", [128, 128], F32)
    ident_b = P.din("ident_bf", [128, 128], BF16)
    tri0 = P.din("tri0", [128, 256], F32)
    tri1 = P.din("tri1", [128, 128], F32)
    hrep = P.din("hrep", [64, 4096], BF16)
    meta_b = P.din("meta_b", [1, 8], I32)

    o_s0 = P.dout("s0", [D, TPC], BF16)

    ag_st_in = P.dint("st_in", [NCH * 128, 4096], F32)
    ag_st_out = P.dint("st_out", [LANES * NCH * 128, 4096], F32)
    ag_cs_in = P.dint("cs_in", [NHEADS, NCH], F32)
    ag_cs_out = P.dint("cs_out", [LANES * NHEADS, NCH], F32)
    inc_scr = P.dint("inc_scr", [GCH * 128, 4096], F32)

    s_xbcc = P.sbuf("s_xbcc", [128, NXC * CH], BF16)
    s_tmv = P.sbuf("s_tmv", [128, NCH * 512], F32)
    s_cs = P.sbuf("s_cs", [64, NCH], F32)
    s_csall = P.sbuf("s_csall", [64, GCH], F32)
    s_E = P.sbuf("s_E", [64, GCH], F32)
    s_cbm = P.sbuf("s_cbm", [128, NCH * 384], BF16)
    s_yfm = P.sbuf("s_yfm", [128, NZC * TPC], BF16)
    s_idf = P.sbuf("s_idf", [128, 128], F32)
    s_idb = P.sbuf("s_idb", [128, 128], BF16)
    s_tri0 = P.sbuf("s_tri0", [128, 256], F32)
    s_tri1 = P.sbuf("s_tri1", [128, 128], F32)
    s_acol = P.sbuf("s_acol", [64, 1], F32)
    s_dfm = P.sbuf("s_dfm", [128, NZC], F32)
    s_nw = P.sbuf("s_nw", [128, NZC], F32)
    s_one = P.sbuf("s_one", [128, 1], F32)
    s_eps = P.sbuf("s_eps", [128, 1], F32)
    s_ones1 = P.sbuf("s_ones1", [1, 128], F32)
    s_ones128 = P.sbuf("s_ones128", [128, 128], F32)
    s_ones128b = P.sbuf("s_ones128b", [128, 128], BF16)
    s_meta = P.sbuf("s_metab", [1, 8], I32)
    AR = 15500
    s_ar = P.sbuf("s_ar", [128, AR], F32)

    def arf(off, rows, cols):
        return s_ar[0:rows, off:off + cols]

    def arb(off, rows, cols):
        return s_ar[:, off:off + (cols + 1) // 2].bitcast(BF16)[0:rows, 0:cols]

    if pp is None:
        pp = PsumPool(P, nbanks=6)
        pb16 = [P.psum(f"px{i}", [128, 1024], BF16) for i in range(2)]
    pb16_guard = [None, None]

    def take16(i):
        g = pb16_guard[i]
        pb16_guard[i] = None
        return pb16[i], ([g] if g else [])

    for t, srcd in ((s_idf, ident_f), (s_idb, ident_b), (s_tri0, tri0),
                    (s_tri1, tri1), (s_acol, A_col),
                    (s_dfm, D_fm), (s_nw, norm_w), (s_meta, meta_b)):
        P.op("sync", "dma_start", out=t[:], in_=srcd[:], inc="cl", incn=16)
    cl_all = P.c("cl")
    P.op("vector", "memset", s_one[:], 1.0)
    P.op("vector", "memset", s_eps[:], LN_EPS)
    P.op("vector", "memset", s_ones1[:], 1.0)
    P.op("vector", "memset", s_ones128[:], 1.0)
    P.op("vector", "memset", s_ones128b[:], 1.0, inc="ve")
    ve_const = P.c("ve")

    xbc_r = i_xbc.rearrange("(c p) t -> p c t", p=128)
    z_r = i_z.rearrange("(c p) t -> p c t", p=128)

    s_dt = arf(0, 64, TPC)
    s_row = arf(1024, 64, 6 * CH)
    s_xctm = arb(2560, 128, 2 * 4096)
    s_xs = [arb(6656 + 64 * i, 128, 128) for i in range(2)]
    s_btm = arb(6784, 128, 256)
    s_stc = [arf(6912 + 512 * i, 128, 512) for i in range(2)]

    P.op("sync", "dma_start", out=s_dt, in_=i_dt[:],
         wait=("cl", cl_all), inc="cl", incn=16)
    cl_all = P.c("cl")

    xbcc_guard = []
    st_states = {}
    stc_guard = [None, None]
    for m in range(NCH):
        w = [("cl", cl_all)] + xbcc_guard
        P.op("sync", "dma_start", out=s_xbcc[:].rearrange("p (c t) -> p c t", t=CH),
             in_=xbc_r[:, :, m * CH:(m + 1) * CH], wait=w, inc="xl", incn=16)
        xl_m = P.c("xl")
        dA = s_row[:, 0:CH]
        ac = s_row[:, CH:2 * CH]
        ds_ = s_row[:, 2 * CH:3 * CH]
        sdo = s_row[:, 3 * CH:4 * CH]
        dtds = s_row[:, 4 * CH:5 * CH]
        tmp = s_row[:, 5 * CH:6 * CH]
        dtc = s_dt[:, m * CH:(m + 1) * CH]
        P.op("vector", "tensor_scalar", dA, dtc, s_acol[:, 0:1], None, ALU.mult,
             wait=[("cl", cl_all), ("ve", ve_const), ("act", P.c("act")),
                   ("pe", P.c("pe"))], inc="ve")
        P.op("vector", "tensor_tensor_scan", ac,
             s_one[0:64, 0:1].broadcast_to([64, CH]), dA, 0.0, ALU.mult, ALU.add,
             wait=("ve", P.c("ve")), inc="ve")
        P.op("vector", "tensor_copy", s_cs[:, m:m + 1], ac[:, CH - 1:CH],
             wait=("ve", P.c("ve")), inc="ve")
        P.op("vector", "tensor_scalar", tmp, ac, s_cs[:, m:m + 1], None, ALU.subtract,
             wait=("ve", P.c("ve")), inc="ve")
        P.op("scalar", "activation", ds_, tmp, AF.Exp, scale=-1.0,
             wait=("ve", P.c("ve")))
        P.op("scalar", "activation", sdo, ac, AF.Exp, inc="act")
        P.op("vector", "tensor_mul", dtds, dtc, ds_, wait=("act", P.c("act")), inc="ve")
        ve_rows = P.c("ve")
        for arr, srcv in ((0, dtc), (1, dtds), (2, sdo), (3, ac)):
            for sc in range(2):
                bnk, pap, g = pp.take(128, 64)
                P.op("tensor", "transpose", pap, srcv[0:64, sc * 128:(sc + 1) * 128],
                     s_idf[0:64, 0:64], wait=[("ve", ve_rows)] + g, inc="pe")
                co = m * 512 + arr * 128 + sc * 64
                P.op("vector", "tensor_copy", s_tmv[:, co:co + 64], pap,
                     wait=("pe", P.c("pe")), inc="ve")
                pp.release(bnk, "ve", P.c("ve"))
        for sc in range(2):
            bnk, pap, g = pp.take(128, 256 if sc == 0 else 128)
            P.op("tensor", "matmul", pap,
                 s_xbcc[:, 32 * CH + sc * 128: 32 * CH + (sc + 1) * 128],
                 s_xbcc[:, 33 * CH + (0 if sc == 0 else 128): 34 * CH],
                 start=True, stop=True, wait=[("xl", xl_m)] + g, inc="pe")
            if sc == 0:
                P.op("vector", "tensor_mul", s_cbm[:, m * 384: m * 384 + 256],
                     pap, s_tri0[:], wait=("pe", P.c("pe")), inc="ve")
            else:
                P.op("vector", "tensor_mul", s_cbm[:, m * 384 + 256: m * 384 + 384],
                     pap, s_tri1[:], wait=("pe", P.c("pe")), inc="ve")
            pp.release(bnk, "ve", P.c("ve"))
        for sc in range(2):
            pt, g = take16(sc)
            P.op("tensor", "transpose", pt[:, 0:128],
                 s_xbcc[:, 32 * CH + sc * 128: 32 * CH + (sc + 1) * 128], s_idb[:],
                 wait=[("xl", xl_m)] + g, inc="pe")
            P.op("vector", "tensor_copy", s_btm[:, sc * 128:(sc + 1) * 128],
                 pt[:, 0:128], wait=("pe", P.c("pe")), inc="ve")
            pb16_guard[sc] = ("ve", P.c("ve"))
        for fc in range(NZC):
            for sc in range(2):
                i16 = (fc * 2 + sc) % 2
                pt, g = take16(i16)
                P.op("tensor", "transpose", pt[:, 0:128],
                     s_xbcc[:, fc * CH + sc * 128: fc * CH + (sc + 1) * 128],
                     s_idb[:], wait=[("xl", xl_m)] + g, inc="pe")
                P.op("vector", "tensor_copy",
                     s_xctm[:, sc * 4096 + fc * 128: sc * 4096 + (fc + 1) * 128],
                     pt[:, 0:128], wait=("pe", P.c("pe")), inc="ve")
                pb16_guard[i16] = ("ve", P.c("ve"))
        for pair in range(32):
            bnk, pap, g = pp.take(128, 128)
            for sc in range(2):
                xb = s_xs[sc]
                for hh in range(2):
                    h = 2 * pair + hh
                    P.op("vector", "tensor_scalar", xb[:, hh * 64:(hh + 1) * 64],
                         s_xctm[:, sc * 4096 + h * 64: sc * 4096 + (h + 1) * 64],
                         s_tmv[:, m * 512 + 128 + sc * 64 + h: m * 512 + 128 + sc * 64 + h + 1],
                         None, ALU.mult, inc="ve" if hh == 1 else None)
                P.op("tensor", "matmul", pap, s_btm[:, sc * 128:(sc + 1) * 128], xb,
                     start=(sc == 0), stop=(sc == 1),
                     wait=[("ve", P.c("ve"))] + (g if sc == 0 else []),
                     inc="pe" if sc == 1 else None)
            si = (pair // 4) % 2
            sg_ = [stc_guard[si]] if (stc_guard[si] and pair % 4 == 0) else []
            P.op("vector", "tensor_copy",
                 s_stc[si][:, (pair % 4) * 128:(pair % 4 + 1) * 128], pap,
                 wait=[("pe", P.c("pe"))] + sg_, inc="ve")
            pp.release(bnk, "ve", P.c("ve"))
            if pair % 4 == 3:
                P.op("sync", "dma_start",
                     out=ag_st_in[m * 128:(m + 1) * 128, (pair - 3) * 128:(pair + 1) * 128],
                     in_=s_stc[si], wait=("ve", P.c("ve")), inc="st", incn=16)
                stc_guard[si] = ("st", P.c("st"))
        st_states[m] = P.c("st")
        xbcc_guard = [("pe", P.c("pe")), ("ve", P.c("ve"))]
    P.op("sync", "dma_start", out=ag_cs_in[:], in_=s_cs[:],
         wait=("ve", P.c("ve")), inc="st", incn=16)
    st_all_contrib = P.c("st")

    P.wait("gpsimd", "st", st_all_contrib)
    P.sem("cc")

    def cc_states(eng):
        eng.collective_compute(
            "AllGather", ALU.bypass,
            replica_groups=[[0, 1, 2, 3], [4, 5, 6, 7]],
            ins=[ag_st_in.opt()], outs=[ag_st_out.opt()]).then_inc(P.sems["cc"], 1)
    P.raw("gpsimd", cc_states); P.bump("cc", 1)

    def cc_cs(eng):
        eng.collective_compute(
            "AllGather", ALU.bypass,
            replica_groups=[[0, 1, 2, 3], [4, 5, 6, 7]],
            ins=[ag_cs_in.opt()], outs=[ag_cs_out.opt()]).then_inc(P.sems["cc"], 1)
    P.raw("gpsimd", cc_cs); P.bump("cc", 1)
    cc_done = P.c("cc")

    P.barrier()
    s_S = arf(0, 128, 4096)
    s_sg = [arf(4096 + 4096 * i, 128, 4096) for i in range(2)]
    s_t = arf(12288, 128, 512)
    s_hrepb = arb(12800, 64, 4096)

    P.op("sync", "dma_start", out=s_hrepb, in_=hrep[:], inc="cl", incn=16)
    cl2 = P.c("cl")
    P.op("vector", "memset", s_S[:], 0.0, inc="ve")
    for r in range(LANES):
        for mi in range(NCH):
            P.op("gpsimd", "dma_start",
                 out=s_csall[:, LANES * mi + r: LANES * mi + r + 1],
                 in_=ag_cs_out[64 * r:64 * (r + 1), mi:mi + 1],
                 wait=("cc", cc_done), inc="hl", incn=16,
                 allow_slow_non_contiguous=True)
    P.op("scalar", "activation", s_E[:], s_csall[:], AF.Exp,
         wait=("hl", P.c("hl")), inc="act")
    act_E = P.c("act")
    s_Sd = arb(14848, 64, 1024)
    for g in range(GCH):
        r, mloc = g % LANES, g // LANES
        P.op("sync", "dma_start", out=inc_scr[g * 128:(g + 1) * 128, :], in_=s_S[:],
             wait=("ve", P.c("ve")), inc="st", incn=16)
        st_save = P.c("st")
        sgb = s_sg[g % 2]
        P.op("sync", "dma_start", out=sgb,
             in_=ag_st_out[r * NCH * 128 + mloc * 128: r * NCH * 128 + (mloc + 1) * 128, :],
             wait=[("cc", cc_done), ("ve", P.c("ve"))], inc="xl", incn=16)
        xl_sg = P.c("xl")
        for j in range(8):
            P.op("vector", "tensor_scalar", s_Sd[0:64, 0:512],
                 s_hrepb[0:64, j * 512:(j + 1) * 512], s_E[:, g:g + 1], None,
                 ALU.mult, wait=[("act", act_E), ("cl", cl2), ("pe", P.c("pe"))],
                 inc="ve")
            ve_sd = P.c("ve")
            bnk, pap, gd = pp.take(128, 512)
            P.op("tensor", "matmul", pap, s_ones128b[0:64, :],
                 s_Sd[0:64, 0:512],
                 start=True, stop=True, wait=[("ve", ve_sd), ("st", st_save)] + gd,
                 inc="pe")
            P.op("vector", "tensor_mul", s_t[:], s_S[:, j * 512:(j + 1) * 512], pap,
                 wait=[("pe", P.c("pe")), ("xl", xl_sg)])
            P.op("vector", "tensor_add", s_S[:, j * 512:(j + 1) * 512], s_t[:],
                 sgb[:, j * 512:(j + 1) * 512], inc="ve")
            pp.release(bnk, "ve", P.c("ve"))
    st_inc_all = P.c("st")

    P.barrier()
    s_xctm3 = arb(0, 128, 2 * 4096)
    s_inc = arb(4096, 128, 4096)
    s_arg = [arf(6144 + 384 * i, 128, 384) for i in range(2)]
    s_mt = [arb(6912 + 192 * i, 128, 384) for i in range(2)]
    s_dg = [arf(7296 + 128 * i, 128, 128) for i in range(2)]
    s_ytm = arb(7552, 128, 8192)
    s_ev = arf(11648, 128, 64)
    s_xdth = arb(11712, 128, 128)

    for m in range(NCH):
        _peg = P.c("pe")
        _veg = P.c("ve")

        def inc_load(eng, m=m, peg=_peg, veg=_veg):
            eng.wait_ge(P.sems["cl"], cl2)
            eng.wait_ge(P.sems["st"], st_inc_all)
            eng.wait_ge(P.sems["pe"], peg)
            eng.wait_ge(P.sems["ve"], veg)
            rb = eng.value_load(s_meta[0:1, m:m + 1])
            eng.dma_start(out=s_inc, in_=inc_scr[DS(rb, 128), :]).then_inc(
                P.sems["hl"], 16)
        P.raw("gpsimd", inc_load)
        P.bump("hl", 16)
        hl_inc = P.c("hl")
        P.op("sync", "dma_start", out=s_xbcc[:].rearrange("p (c t) -> p c t", t=CH),
             in_=xbc_r[:, :, m * CH:(m + 1) * CH],
             wait=[("pe", P.c("pe")), ("ve", P.c("ve"))], inc="xl", incn=16)
        xl_xb = P.c("xl")
        for fc in range(NZC):
            for sc in range(2):
                i16 = (fc * 2 + sc) % 2
                pt, g = take16(i16)
                P.op("tensor", "transpose", pt[:, 0:128],
                     s_xbcc[:, fc * CH + sc * 128: fc * CH + (sc + 1) * 128],
                     s_idb[:], wait=[("xl", xl_xb)] + g, inc="pe")
                P.op("vector", "tensor_copy",
                     s_xctm3[:, sc * 4096 + fc * 128: sc * 4096 + (fc + 1) * 128],
                     pt[:, 0:128], wait=("pe", P.c("pe")), inc="ve")
                pb16_guard[i16] = ("ve", P.c("ve"))
        ve_xctm = P.c("ve")
        dgp = [None, None]
        for pair in range(32):
            bnk, pyd, gy = pp.take(128, 512)
            for lc in range(2):
                P.op("tensor", "matmul", pyd[:, 256 + lc * 128: 384 + lc * 128],
                     s_xbcc[:, 33 * CH + lc * 128: 33 * CH + (lc + 1) * 128],
                     s_inc[:, pair * 128:(pair + 1) * 128],
                     start=True, stop=True,
                     wait=([("hl", hl_inc), ("xl", xl_xb)] + gy) if lc == 0 else None,
                     inc="pe" if lc == 1 else None)
            for hh in range(2):
                h = 2 * pair + hh
                ai = h % 2
                for lc in range(2):
                    P.op("gpsimd", "tensor_scalar", s_dg[ai], s_idf[:],
                         s_tmv[:, m * 512 + 384 + lc * 64 + h: m * 512 + 384 + lc * 64 + h + 1],
                         None, ALU.mult,
                         wait=([("pe", dgp[ai])] if dgp[ai] else [("ve", ve_xctm)]),
                         inc="pl")
                    bnk2, pab, g2 = pp.take(128, 128)
                    P.op("tensor", "matmul", pab, s_ones128[:], s_dg[ai],
                         start=True, stop=True, wait=[("pl", P.c("pl"))] + g2,
                         inc="pe")
                    dgp[ai] = P.c("pe")
                    if lc == 0:
                        P.op("vector", "tensor_scalar", s_arg[ai][:, 0:128], pab,
                             s_tmv[:, m * 512 + 384 + h: m * 512 + 384 + h + 1],
                             0.0, ALU.subtract, ALU.min,
                             wait=("pe", dgp[ai]), inc="ve")
                    else:
                        P.op("vector", "tensor_scalar", s_arg[ai][:, 128:256], pab,
                             s_tmv[:, m * 512 + 384 + h: m * 512 + 384 + h + 1],
                             0.0, ALU.subtract, ALU.min,
                             wait=("pe", dgp[ai]))
                        P.op("vector", "tensor_scalar", s_arg[ai][:, 256:384], pab,
                             s_tmv[:, m * 512 + 384 + 64 + h: m * 512 + 384 + 64 + h + 1],
                             0.0, ALU.subtract, ALU.min, inc="ve")
                    pp.release(bnk2, "ve", P.c("ve"))
                P.op("scalar", "activation", s_arg[ai][:], s_arg[ai][:], AF.Exp,
                     wait=("ve", P.c("ve")), inc="act")
                P.op("vector", "tensor_mul", s_mt[ai], s_arg[ai][:],
                     s_cbm[:, m * 384:(m + 1) * 384], wait=("act", P.c("act")),
                     inc="ve")
                ve_mt = P.c("ve")
                for sc in range(2):
                    P.op("vector", "tensor_scalar", s_xdth[:, sc * 64:(sc + 1) * 64],
                         s_xctm3[:, sc * 4096 + h * 64: sc * 4096 + (h + 1) * 64],
                         s_tmv[:, m * 512 + sc * 64 + h: m * 512 + sc * 64 + h + 1],
                         None, ALU.mult, inc="ve" if sc == 1 else None)
                ve_xdt = P.c("ve")
                P.op("tensor", "matmul", pyd[:, hh * 64: hh * 64 + 64],
                     s_mt[ai][:, 0:128], s_xdth[:, 0:64],
                     start=True, stop=True, wait=("ve", ve_xdt))
                P.op("tensor", "matmul", pyd[:, 128 + hh * 64: 128 + hh * 64 + 64],
                     s_mt[ai][:, 128:256], s_xdth[:, 0:64],
                     start=True, stop=False)
                P.op("tensor", "matmul", pyd[:, 128 + hh * 64: 128 + hh * 64 + 64],
                     s_mt[ai][:, 256:384], s_xdth[:, 64:128],
                     start=False, stop=True, inc="pe")
            pe_y = P.c("pe")
            for lc in range(2):
                for hh in range(2):
                    h = 2 * pair + hh
                    sdoc = s_tmv[:, m * 512 + 256 + lc * 64 + h: m * 512 + 256 + lc * 64 + h + 1]
                    P.op("vector", "tensor_scalar", s_ev,
                         pyd[:, 256 + lc * 128 + hh * 64: 256 + lc * 128 + hh * 64 + 64],
                         sdoc, None, ALU.mult, wait=("pe", pe_y))
                    P.op("vector", "tensor_add",
                         s_ytm[:, lc * 4096 + pair * 128 + hh * 64: lc * 4096 + pair * 128 + hh * 64 + 64],
                         s_ev,
                         pyd[:, lc * 128 + hh * 64: lc * 128 + hh * 64 + 64],
                         inc="ve")
            pp.release(bnk, "ve", P.c("ve"))
        ve_ytm = P.c("ve")
        for fc in range(NZC):
            for lc in range(2):
                i16 = (fc * 2 + lc) % 2
                pt, g = take16(i16)
                P.op("tensor", "transpose", pt[:, 0:128],
                     s_ytm[:, lc * 4096 + fc * 128: lc * 4096 + (fc + 1) * 128],
                     s_idb[:], wait=[("ve", ve_ytm)] + g, inc="pe")
                P.op("vector", "scalar_tensor_tensor",
                     s_yfm[:, fc * TPC + m * CH + lc * 128: fc * TPC + m * CH + (lc + 1) * 128],
                     s_xbcc[:, fc * CH + lc * 128: fc * CH + (lc + 1) * 128],
                     s_dfm[:, fc:fc + 1], pt[:, 0:128],
                     ALU.mult, ALU.add, wait=("pe", P.c("pe")), inc="ve")
                pb16_guard[i16] = ("ve", P.c("ve"))
    ve_yfm = P.c("ve")

    P.barrier()
    s_zt = arb(0, 128, 512)
    s_sq = arf(256, 128, 512)
    s_rr = arf(768, 1, 1024)
    for tt in range(2):
        bnkr, papr, gr = pp.take(1, 512)
        for fc in range(NZC):
            P.op("scalar", "dma_start", out=s_zt,
                 in_=z_r[:, fc, tt * 512:(tt + 1) * 512],
                 wait=[("ve", P.c("ve")), ("act", P.c("act"))], inc="zl", incn=16)
            P.op("scalar", "activation", s_zt, s_zt, AF.Silu,
                 wait=("zl", P.c("zl")), inc="act")
            yf = s_yfm[:, fc * TPC + tt * 512: fc * TPC + (tt + 1) * 512]
            P.op("vector", "tensor_mul", yf, yf, s_zt,
                 wait=[("act", P.c("act")), ("ve", ve_yfm)], inc="ve")
            P.op("scalar", "activation", s_sq, yf, AF.Square,
                 wait=[("ve", P.c("ve")), ("pe", P.c("pe"))], inc="act")
            P.op("tensor", "matmul", papr, s_one[:, 0:1], s_sq,
                 start=(fc == 0), stop=(fc == NZC - 1),
                 wait=[("act", P.c("act"))] + (gr if fc == 0 else []),
                 inc="pe")
        rr = s_rr[:, tt * 512:(tt + 1) * 512]
        P.op("vector", "tensor_scalar", rr, papr[0:1, :], 1.0 / D_INNER, None,
             ALU.mult, wait=("pe", P.c("pe")), inc="ve")
        pp.release(bnkr, "ve", P.c("ve"))
        P.op("scalar", "activation", rr, rr, AF.Ln, bias=s_eps[0:1, 0:1],
             wait=("ve", P.c("ve")))
        P.op("scalar", "activation", rr, rr, AF.Exp, scale=-0.5, inc="act")
        bnkb, papb, gb = pp.take(128, 512)
        P.op("tensor", "matmul", papb, s_ones1[:], rr, start=True, stop=True,
             wait=[("act", P.c("act"))] + gb, inc="pe")
        pe_bc = P.c("pe")
        for fc in range(NZC):
            yf = s_yfm[:, fc * TPC + tt * 512: fc * TPC + (tt + 1) * 512]
            P.op("vector", "scalar_tensor_tensor", yf, yf, s_nw[:, fc:fc + 1],
                 papb, ALU.mult, ALU.mult, wait=("pe", pe_bc),
                 inc="ve" if fc == NZC - 1 else None)
        pp.release(bnkb, "ve", P.c("ve"))
    ve_yn = P.c("ve")

    P.barrier()
    s_wo = [arb(0 + 2048 * i, 128, 4096) for i in range(2)]
    s_st0 = [arb(4096 + 256 * i, 128, 512) for i in range(2)]
    st0_guard = [None, None]
    pe_wo = {}
    for mb in range(16):
        bi = mb % 2
        w = [("pe", pe_wo[mb - 2])] if mb >= 2 else []
        P.op("gpsimd", "dma_start",
             out=s_wo[bi].rearrange("p (k o) -> p k o", o=128),
             in_=w_out[mb], wait=w, inc="wl", incn=16)
        wl_m = P.c("wl")
        for tt in range(2):
            bnk, pap, g = pp.take(128, 512)
            for k in range(NZC):
                P.op("tensor", "matmul", pap, s_wo[bi][:, k * 128:(k + 1) * 128],
                     s_yfm[:, k * TPC + tt * 512: k * TPC + tt * 512 + 512],
                     start=(k == 0), stop=(k == NZC - 1),
                     wait=([("wl", wl_m), ("ve", ve_yn)] + g) if k == 0 else None,
                     inc="pe" if k == NZC - 1 else None)
            sgd = [st0_guard[tt]] if st0_guard[tt] else []
            P.op("vector", "tensor_copy", s_st0[tt], pap,
                 wait=[("pe", P.c("pe"))] + sgd, inc="ve")
            P.op("scalar", "dma_start",
                 out=o_s0[mb * 128:(mb + 1) * 128, tt * 512:(tt + 1) * 512],
                 in_=s_st0[tt], wait=("ve", P.c("ve")), inc="st", incn=16)
            st0_guard[tt] = ("st", P.c("st"))
            pp.release(bnk, "ve", P.c("ve"))
        pe_wo[mb] = P.c("pe")
    P.wait("gpsimd", "st", P.c("st"))
    return P





N_ICG = 88
N_ICS = 11
LN2 = 0.6931471805599453


def build_C(P=None, pp=None, pb16=None, debug=False, stop_after=None):
    if P is None:
        P = Prog("C")
    P.prefix = "C_"

    i_xn = P.din("xn", [D, TPC], BF16)
    i_s0 = P.din("s0", [D, TPC], BF16)
    i_xfm = P.din("x_fm", [D, TPC], F32)
    gup = P.din("gup_s", [1408, 256], I32)
    gusc = P.din("gusc", [128, 176], mybir.dt.float16)
    dnp = P.din("dnp_s", [256, 704], I32)
    dnsc = P.din("dnsc", [128, 88], mybir.dt.float16)
    ident_b = P.din("ident_bf", [128, 128], BF16)

    o_x1 = P.dout("x1", [D, TPC], F32)
    o_g0 = P.dout("g0", [D, TPC], BF16)
    o_dbg = P.dout("dbg", [128, 2048], BF16) if debug else None
    o_dbg2 = P.dout("dbg2", [128, 1024], BF16) if debug else None

    ag_gu_in = P.dint("gu_in", [1408, D], BF16)
    ag_gu_out = P.dint("gu_out", [NCORES * 1408, D], BF16)
    ag_dn_in = P.dint("dn_in", [256, I_FFN], BF16)
    ag_dn_out = P.dint("dn_out", [NCORES * 256, I_FFN], BF16)

    s_xn = P.sbuf("s_xn", [128, NKC * TPC], BF16)
    s_idb = P.sbuf("s_idb", [128, 128], BF16)
    s_gup = P.sbuf("s_gup", [128, N_ICS * 256], I32)
    s_gusc = P.sbuf("s_gusc", [128, 176], mybir.dt.float16)
    s_dnp = P.sbuf("s_dnp", [128, 2 * 704], I32)
    s_dnsc = P.sbuf("s_dnsc", [128, 88], mybir.dt.float16)
    s_dq = P.sbuf("s_dq", [128, 8 * 352], F32)
    s_scx = P.sbuf("s_scx", [128, 352], F32)
    s_wdq = P.sbuf("s_wdq", [128, 2048], BF16)
    s_wdqd = P.sbuf("s_wdqd", [128, 2816], BF16)
    s_guT = [P.sbuf(f"s_guT{i}", [128, 2048], BF16) for i in range(2)]
    s_dnT = P.sbuf("s_dnT", [128, I_FFN], BF16)
    s_wg = [P.sbuf(f"s_wg{i}", [128, NKC * 128], BF16) for i in range(2)]
    s_wu = [P.sbuf(f"s_wu{i}", [128, NKC * 128], BF16) for i in range(2)]
    s_wd = [P.sbuf(f"s_wd{i}", [128, 44 * 128], BF16) for i in range(2)]
    s_g = P.sbuf("s_g", [128, 512], BF16)
    s_h = P.sbuf("s_h", [128, 44 * 512], BF16)
    s_y = P.sbuf("s_y", [128, NKC * 512], BF16)
    s_sq = P.sbuf("s_sq", [128, 512], F32)
    s_rr = P.sbuf("s_rr", [1, 512], F32)
    s_xf = P.sbuf("s_xf", [128, 512], F32)
    s_sg0 = P.sbuf("s_sg0", [128, 512], BF16)
    s_x1 = [P.sbuf(f"s_x1{i}", [128, 512], F32) for i in range(2)]
    s_one = P.sbuf("s_one", [128, 1], F32)
    s_ones1 = P.sbuf("s_ones1c", [1, 128], F32)
    s_nln2 = P.sbuf("s_nln2", [128, 1], F32)
    s_epsf = P.sbuf("s_epsf", [128, 1], F32)

    if pp is None:
        pp = PsumPool(P, nbanks=6)
        pb16 = [P.psum(f"px{i}", [128, 1024], BF16) for i in range(2)]
    pb16_guard = [None, None]

    def take16(i):
        g = pb16_guard[i]
        pb16_guard[i] = None
        return pb16[i], ([g] if g else [])

    P.op("sync", "dma_start", out=s_idb[:], in_=ident_b[:], inc="cl", incn=16)
    P.op("sync", "dma_start", out=s_gup[:].rearrange("p (c w) -> p c w", w=256),
         in_=gup.rearrange("(c p) w -> p c w", p=128), inc="cl", incn=16)
    P.op("sync", "dma_start", out=s_gusc[:], in_=gusc[:], inc="cl", incn=16)
    P.op("sync", "dma_start", out=s_dnp[:].rearrange("p (c w) -> p c w", w=704),
         in_=dnp.rearrange("(c p) w -> p c w", p=128), inc="cl", incn=16)
    P.op("sync", "dma_start", out=s_dnsc[:], in_=dnsc[:], inc="cl", incn=16)
    xn_src = i_xn.rearrange("(k p) t -> p k t", p=128)
    P.op("sync", "dma_start", out=s_xn[:].rearrange("p (k t) -> p k t", t=TPC),
         in_=xn_src, inc="cl", incn=16)
    cl_all = P.c("cl")
    P.op("vector", "memset", s_one[:], 1.0)
    P.op("vector", "memset", s_ones1[:], 1.0)
    P.op("vector", "memset", s_nln2[:], -LN2)
    P.op("vector", "memset", s_epsf[:], 1e-6, inc="ve")
    ve_const = P.c("ve")

    def dequant(words, nwords, scol_base, sc_n, out_tile, wait_first):
        W = nwords
        def sl(i):
            return s_dq[:, i * W:(i + 1) * W]
        def sli(i):
            return s_dq[:, i * W:(i + 1) * W].bitcast(I32)
        P.op("vector", "tensor_copy",
             s_scx[:, 0:W].rearrange("p (g u) -> p g u", u=W // sc_n),
             scol_base.broadcast_to([128, sc_n, W // sc_n]),
             wait=wait_first, inc="ve")
        for r in range(8):
            w0 = [("ve", P.c("ve"))] if r == 0 else None
            n = sli(0); k = sli(1); m = sli(2); e2 = sli(3)
            arg = sl(4); fix = sl(5); t0 = sl(6); mag = sl(7)
            P.op("vector", "tensor_scalar", n, words, 4 * r, 15,
                 ALU.logical_shift_right, ALU.bitwise_and, wait=w0)
            P.op("vector", "tensor_scalar", k, n, 7, None, ALU.bitwise_and)
            P.op("vector", "tensor_scalar", m, n, 1, None, ALU.bitwise_and)
            P.op("vector", "tensor_scalar", e2, k, 1, None, ALU.logical_shift_right)
            P.op("vector", "scalar_tensor_tensor", arg, m, 0.5849625007211562, e2,
                 ALU.mult, ALU.add, inc="ve")
            P.op("scalar", "activation", mag, arg, AF.Exp, scale=LN2,
                 bias=s_nln2[:, 0:1], wait=("ve", P.c("ve")), inc="act")
            P.op("vector", "tensor_scalar", t0, k, 0, None, ALU.is_equal)
            P.op("vector", "tensor_scalar", t0, t0, -1.0, 1.0, ALU.mult, ALU.add)
            P.op("vector", "tensor_scalar", fix, k, 1, None, ALU.is_equal)
            P.op("vector", "scalar_tensor_tensor", fix, fix, -1.0 / 3.0, t0,
                 ALU.mult, ALU.add)
            P.op("vector", "tensor_scalar", sli(6), n, 8, None, ALU.bitwise_and)
            P.op("vector", "tensor_scalar", t0, sli(6), -0.25, 1.0, ALU.mult, ALU.add)
            P.op("vector", "tensor_mul", fix, fix, t0)
            P.op("vector", "tensor_mul", fix, fix, s_scx[:, 0:W],
                 wait=("act", P.c("act")))
            P.op("vector", "tensor_mul",
                 out_tile.rearrange("p (j r) -> p j r", r=8)[:, :, r],
                 mag, fix, inc="ve")
        return P.c("ve")

    guT_guard = [None, None]
    for ic in range(N_ICS):
        wfirst = [("cl", cl_all), ("ve", ve_const)] if ic == 0 else [("ve", P.c("ve"))]
        ve_dq = dequant(s_gup[:, ic * 256:(ic + 1) * 256], 256,
                        s_gusc[:, ic * 16:(ic + 1) * 16], 16, s_wdq[:], wfirst)
        bi = ic % 2
        if debug and ic == 0:
            P.op("scalar", "dma_start", out=o_dbg[:], in_=s_wdq[:],
                 wait=("ve", ve_dq), inc="st", incn=16)
        gg = [guT_guard[bi]] if guT_guard[bi] else []
        for kc in range(NKC):
            pt, g = take16(kc % 2)
            P.op("tensor", "transpose", pt[:, 0:128],
                 s_wdq[:, kc * 128:(kc + 1) * 128], s_idb[:],
                 wait=[("ve", ve_dq)] + g, inc="pe")
            P.op("vector", "tensor_copy", s_guT[bi][:, kc * 128:(kc + 1) * 128],
                 pt[:, 0:128], wait=[("pe", P.c("pe"))] + (gg if kc == 0 else []),
                 inc="ve")
            pb16_guard[kc % 2] = ("ve", P.c("ve"))
        P.op("sync", "dma_start",
             out=ag_gu_in[ic * 128:(ic + 1) * 128, :],
             in_=s_guT[bi][:], wait=("ve", P.c("ve")), inc="st", incn=16)
        guT_guard[bi] = ("st", P.c("st"))
    for hc in range(2):
        for half in range(2):
            ve_dq = dequant(s_dnp[:, hc * 704 + half * 352: hc * 704 + (half + 1) * 352],
                            352, s_dnsc[:, hc * 44 + half * 22: hc * 44 + (half + 1) * 22],
                            22, s_wdqd[:], [("ve", P.c("ve"))])
            for icl in range(22):
                icc = half * 22 + icl
                pt, g = take16(icl % 2)
                P.op("tensor", "transpose", pt[:, 0:128],
                     s_wdqd[:, icl * 128:(icl + 1) * 128], s_idb[:],
                     wait=[("ve", ve_dq)] + g, inc="pe")
                P.op("vector", "tensor_copy", s_dnT[:, icc * 128:(icc + 1) * 128],
                     pt[:, 0:128], wait=("pe", P.c("pe")), inc="ve")
                pb16_guard[icl % 2] = ("ve", P.c("ve"))
        P.op("sync", "dma_start",
             out=ag_dn_in[hc * 128:(hc + 1) * 128, :],
             in_=s_dnT[:], wait=("ve", P.c("ve")), inc="st", incn=16)
    st_contrib = P.c("st")

    if stop_after == "dequant":
        P.wait("gpsimd", "st", P.c("st"))
        return P
    P.wait("gpsimd", "st", st_contrib)
    P.sem("cc")

    def cc_gu(eng):
        eng.collective_compute(
            "AllGather", ALU.bypass, replica_groups=[list(range(NCORES))],
            ins=[ag_gu_in.opt()], outs=[ag_gu_out.opt()]).then_inc(P.sems["cc"], 1)
    P.raw("gpsimd", cc_gu); P.bump("cc", 1)

    def cc_dn(eng):
        eng.collective_compute(
            "AllGather", ALU.bypass, replica_groups=[list(range(NCORES))],
            ins=[ag_dn_in.opt()], outs=[ag_dn_out.opt()]).then_inc(P.sems["cc"], 1)
    P.raw("gpsimd", cc_dn); P.bump("cc", 1)
    cc_done = P.c("cc")

    if stop_after == "ag":
        def cc_wait(eng):
            eng.wait_ge(P.sems["cc"], cc_done)
        P.raw("gpsimd", cc_wait)
        P.wait("gpsimd", "st", P.c("st"))
        return P
    x1_r = o_x1.rearrange("(k p) t -> p k t", p=128)
    xfm_r = i_xfm.rearrange("(k p) t -> p k t", p=128)
    s0_r = i_s0.rearrange("(k p) t -> p k t", p=128)
    g0_r = o_g0.rearrange("(k p) t -> p k t", p=128)
    wg_guard = [None, None]
    wu_guard = [None, None]
    wd_guard = [None, None]
    pe_prev_down = 0
    act_prev_apply = 0
    st_prev_apply = 0
    for tt in range(2):
        for mi in range(44):
            bi = mi % 2
            icg_g = mi
            icg_u = mi + 44
            if tt == 0 or True:
                wwg = [wg_guard[bi]] if wg_guard[bi] else []
                r, icl = icg_g // N_ICS, icg_g % N_ICS
                P.op("gpsimd", "dma_start", out=s_wg[bi][:],
                     in_=ag_gu_out[r * 1408 + icl * 128: r * 1408 + (icl + 1) * 128, :],
                     wait=[("cc", cc_done)] + wwg, inc="wl", incn=16)
                r, icl = icg_u // N_ICS, icg_u % N_ICS
                P.op("gpsimd", "dma_start", out=s_wu[bi][:],
                     in_=ag_gu_out[r * 1408 + icl * 128: r * 1408 + (icl + 1) * 128, :],
                     inc="wl", incn=16)
            wl_m = P.c("wl")
            bnkg, papg, gg = pp.take(128, 512)
            bnku, papu, gu_ = pp.take(128, 512)
            for k in range(NKC):
                P.op("tensor", "matmul", papg, s_wg[bi][:, k * 128:(k + 1) * 128],
                     s_xn[:, k * TPC + tt * 512: k * TPC + tt * 512 + 512],
                     start=(k == 0), stop=(k == NKC - 1),
                     wait=([("wl", wl_m), ("cl", cl_all)] + gg) if k == 0 else None,
                     inc="pe" if k == NKC - 1 else None)
            for k in range(NKC):
                P.op("tensor", "matmul", papu, s_wu[bi][:, k * 128:(k + 1) * 128],
                     s_xn[:, k * TPC + tt * 512: k * TPC + tt * 512 + 512],
                     start=(k == 0), stop=(k == NKC - 1),
                     wait=(gu_) if (k == 0 and gu_) else None,
                     inc="pe" if k == NKC - 1 else None)
            pe_here = P.c("pe")
            P.op("scalar", "activation", s_g[:], papg, AF.Silu,
                 wait=[("pe", pe_here), ("ve", P.c("ve"))], inc="act")
            P.op("vector", "tensor_mul", s_h[:, mi * 512:(mi + 1) * 512],
                 s_g[:], papu,
                 wait=[("act", P.c("act")), ("pe", pe_prev_down)], inc="ve")
            if debug and tt == 0 and mi == 0:
                P.op("scalar", "dma_start", out=o_dbg2[:, 0:512],
                     in_=s_h[:, 0:512], wait=("ve", P.c("ve")), inc="st", incn=16)
            pp.release(bnkg, "ve", P.c("ve"))
            pp.release(bnku, "ve", P.c("ve"))
            wg_guard[bi] = ("pe", pe_here)
            wu_guard[bi] = ("pe", pe_here)
        ve_h = P.c("ve")
        if stop_after == "tt1h" and tt == 1:
            P.wait("gpsimd", "ve", ve_h)
            P.wait("gpsimd", "st", P.c("st"))
            return P
        if stop_after == "h" and tt == 0:
            P.wait("gpsimd", "st", P.c("st"))
            P.wait("gpsimd", "ve", ve_h)
            return P
        for mb in range(NKC):
            bi = mb % 2
            wwd = [wd_guard[bi]] if wd_guard[bi] else []
            P.op("gpsimd", "dma_start", out=s_wd[bi][:],
                 in_=ag_dn_out[(mb // 2) * 256 + (mb % 2) * 128:
                               (mb // 2) * 256 + (mb % 2) * 128 + 128, :],
                 wait=[("cc", cc_done)] + wwd, inc="wl", incn=16)
            wl_m = P.c("wl")
            bnk, pap, g = pp.take(128, 512)
            for ki in range(44):
                P.op("tensor", "matmul", pap, s_wd[bi][:, ki * 128:(ki + 1) * 128],
                     s_h[:, ki * 512:(ki + 1) * 512],
                     start=(ki == 0), stop=(ki == 43),
                     wait=([("wl", wl_m), ("ve", ve_h)] + g) if ki == 0 else None,
                     inc="pe" if ki == 43 else None)
            P.op("vector", "tensor_copy", s_y[:, mb * 512:(mb + 1) * 512], pap,
                 wait=[("pe", P.c("pe")), ("act", act_prev_apply),
                       ("st", st_prev_apply)], inc="ve")
            if debug and tt == 0 and mb == 0:
                P.op("scalar", "dma_start", out=o_dbg2[:, 512:1024],
                     in_=s_y[:, 0:512], wait=("ve", P.c("ve")), inc="st", incn=16)
            pp.release(bnk, "ve", P.c("ve"))
            wd_guard[bi] = ("pe", P.c("pe"))
        ve_y = P.c("ve")
        pe_prev_down = P.c("pe")
        if stop_after == "tt1down" and tt == 1:
            P.wait("gpsimd", "ve", ve_y)
            P.wait("gpsimd", "st", P.c("st"))
            return P
        if stop_after == "down" and tt == 0:
            P.wait("gpsimd", "ve", ve_y)
            P.wait("gpsimd", "st", P.c("st"))
            return P
        bnkr, papr, gr = pp.take(1, 512)
        for mb in range(NKC):
            P.op("scalar", "activation", s_sq[:], s_y[:, mb * 512:(mb + 1) * 512],
                 AF.Square, wait=[("ve", ve_y), ("pe", P.c("pe"))], inc="act")
            P.op("tensor", "matmul", papr, s_one[:, 0:1], s_sq[:],
                 start=(mb == 0), stop=(mb == NKC - 1),
                 wait=[("act", P.c("act"))] + (gr if mb == 0 else []), inc="pe")
        P.op("vector", "tensor_scalar", s_rr[:], papr[0:1, :], 1.0 / D, None,
             ALU.mult, wait=("pe", P.c("pe")), inc="ve")
        pp.release(bnkr, "ve", P.c("ve"))
        P.op("scalar", "activation", s_rr[:], s_rr[:], AF.Ln, bias=s_epsf[0:1, 0:1],
             wait=("ve", P.c("ve")))
        P.op("scalar", "activation", s_rr[:], s_rr[:], AF.Exp, scale=-0.5, inc="act")
        bnkb, papb, gb = pp.take(128, 512)
        P.op("tensor", "matmul", papb, s_ones1[:],
             s_rr[:], start=True, stop=True, wait=[("act", P.c("act"))] + gb,
             inc="pe")
        pe_bc = P.c("pe")
        if stop_after == "rstd" and tt == 0:
            P.wait("gpsimd", "pe", pe_bc)
            P.wait("gpsimd", "st", P.c("st"))
            return P
        x1_guard = [None, None]
        for mb in range(NKC):
            bx = mb % 2
            P.op("vector", "tensor_mul", s_y[:, mb * 512:(mb + 1) * 512],
                 s_y[:, mb * 512:(mb + 1) * 512], papb, wait=("pe", pe_bc), inc="ve")
            P.op("scalar", "activation", s_y[:, mb * 512:(mb + 1) * 512],
                 s_y[:, mb * 512:(mb + 1) * 512], AF.Silu,
                 wait=("ve", P.c("ve")), inc="act")
            act_g = P.c("act")
            P.op("scalar", "dma_start", out=g0_r[:, mb, tt * 512:(tt + 1) * 512],
                 in_=s_y[:, mb * 512:(mb + 1) * 512], inc="st", incn=16)
            if stop_after == "apply1" and tt == 0:
                P.wait("gpsimd", "st", P.c("st"))
                P.wait("gpsimd", "act", P.c("act"))
                return P
            P.op("sync", "dma_start", out=s_xf[:],
                 in_=xfm_r[:, mb, tt * 512:(tt + 1) * 512],
                 wait=[("ve", P.c("ve"))], inc="xl", incn=16)
            P.op("sync", "dma_start", out=s_sg0[:],
                 in_=s0_r[:, mb, tt * 512:(tt + 1) * 512], inc="xl", incn=16)
            xl_c = P.c("xl")
            xg = [x1_guard[bx]] if x1_guard[bx] else []
            P.op("vector", "tensor_mul", s_x1[bx][:], s_sg0[:],
                 s_y[:, mb * 512:(mb + 1) * 512],
                 wait=[("xl", xl_c), ("act", act_g)] + xg)
            P.op("vector", "tensor_add", s_x1[bx][:], s_x1[bx][:], s_xf[:], inc="ve")
            P.op("scalar", "dma_start", out=x1_r[:, mb, tt * 512:(tt + 1) * 512],
                 in_=s_x1[bx][:], wait=("ve", P.c("ve")), inc="st", incn=16)
            x1_guard[bx] = ("st", P.c("st"))
            if stop_after == "comb1" and tt == 0 and mb == 0:
                P.wait("gpsimd", "st", P.c("st"))
                return P
        if stop_after == "tt0" and tt == 0:
            P.wait("gpsimd", "st", P.c("st"))
            return P
    P.wait("gpsimd", "st", P.c("st"))
    return P




def build_D(P=None, pp=None, pb16=None, debug=False):
    if P is None:
        P = Prog("D")
    P.prefix = "D_"

    i_x1 = P.din("x1", [D, TPC], F32)
    wq_t = P.din("wq_b", [16, 128, NKC, 128], F32)
    wk_t = P.din("wk_b", [4, 128, NKC, 128], F32)
    wv_t = P.din("wv_b", [4, 128, NKC, 128], F32)
    wo_t = P.din("wo_b", [16, 128, NKC, 128], F32)
    cos_t = P.din("cos_t", [128, TPC], F32)
    csb_t = P.din("csb_t", [128, TPC], F32)
    cdm0 = P.din("cdm0", [128, 256], F32)
    cdm1 = P.din("cdm1", [128, 256], F32)
    ident_b = P.din("ident_bf", [128, 128], BF16)

    o_s1 = P.dout("s1", [D, TPC], BF16)
    o_xn2 = P.dout("xn2", [D, TPC], BF16)
    o_dq = P.dout("dq", [D, TPC], BF16) if debug else None
    o_dk = P.dout("dk", [HKV * HD, TPC], BF16) if debug else None
    o_dy = P.dout("dy", [D, TPC], BF16) if debug else None
    o_dkf = P.dout("dkf", [4 * 128, 4096], BF16) if debug else None
    o_dvt = P.dout("dvt", [32 * 128, 520], BF16) if debug else None

    ag_kv_in = P.dint("kv_in", [1536, TPC], BF16)
    ag_kv_out = P.dint("kv_out", [LANES * 1536, TPC], BF16)

    s_q = P.sbuf("s_q", [128, NKC * TPC], BF16)
    s_y = P.sbuf("s_y", [128, NKC * TPC], BF16)
    s_cos = P.sbuf("s_cos", [128, TPC], F32)
    s_csb = P.sbuf("s_csb", [128, TPC], F32)
    s_cdm0 = P.sbuf("s_cdm0", [128, 256], F32)
    s_cdm1 = P.sbuf("s_cdm1", [128, 256], F32)
    s_idb = P.sbuf("s_idb", [128, 128], BF16)
    s_one = P.sbuf("s_one", [128, 1], F32)
    s_eps = P.sbuf("s_eps", [128, 1], F32)
    s_ones1 = P.sbuf("s_ones1", [1, 128], F32)
    s_rows = P.sbuf("s_rows", [1, 8 * 512], F32)
    s_kf = P.sbuf("s_kf", [128, 4 * 4096], BF16)

    AR = 19000
    s_ar = P.sbuf("s_ar", [128, AR], F32)

    def arf(off, rows, cols):
        return s_ar[0:rows, off:off + cols]

    def arb(off, rows, cols):
        return s_ar[:, off:off + (cols + 1) // 2].bitcast(BF16)[0:rows, 0:cols]

    if pp is None:
        pp = PsumPool(P, nbanks=6)
        pb16 = [P.psum(f"px{i}", [128, 1024], BF16) for i in range(2)]
    pb16_guard = [None, None]

    def take16(i):
        g = pb16_guard[i]
        pb16_guard[i] = None
        return pb16[i], ([g] if g else [])

    for t, srcd in ((s_cos, cos_t), (s_csb, csb_t), (s_cdm0, cdm0),
                    (s_cdm1, cdm1), (s_idb, ident_b)):
        P.op("sync", "dma_start", out=t[:], in_=srcd[:], inc="cl", incn=16)
    cl_all = P.c("cl")
    P.op("vector", "memset", s_one[:], 1.0)
    P.op("vector", "memset", s_ones1[:], 1.0)
    P.op("vector", "memset", s_eps[:], LN_EPS, inc="ve")
    ve_const = P.c("ve")

    s_xn2 = arb(0, 128, NKC * TPC)
    s_xt = [arf(8192 + 512 * i, 128, 512) for i in range(2)]
    s_sq = arf(9216, 128, 512)
    s_w = [arb(9728 + 1024 * i, 128, NKC * 128) for i in range(2)]
    s_qraw = arb(11776, 128, TPC)
    s_qsw = arb(12288, 128, TPC)
    s_k = arb(12800, 128, 4 * TPC)
    s_v = arb(14848, 128, 4 * TPC)
    s_vtm = arb(16896, 128, 4 * TPC)

    x1_r = i_x1.rearrange("(k p) t -> p k t", p=128)
    xn2_r = o_xn2.rearrange("(k p) t -> p k t", p=128)

    xt_guard = [None, None]
    for tt in range(2):
        bnk1, pap1, g1 = pp.take(1, 512)
        bnk2, pap2, g2 = pp.take(1, 512)
        for kc in range(NKC):
            bi = kc % 2
            w = [("cl", cl_all)] + ([xt_guard[bi]] if xt_guard[bi] else [])
            P.op("sync", "dma_start", out=s_xt[bi],
                 in_=x1_r[:, kc, tt * 512:(tt + 1) * 512], wait=w,
                 inc="xl", incn=16)
            xl_t = P.c("xl")
            P.op("tensor", "matmul", pap1, s_one[:, 0:1], s_xt[bi],
                 start=(kc == 0), stop=(kc == NKC - 1),
                 wait=[("xl", xl_t)] + (g1 if kc == 0 else []), inc="pe")
            P.op("scalar", "activation", s_sq, s_xt[bi], AF.Square,
                 wait=("xl", xl_t), inc="act")
            P.op("tensor", "matmul", pap2, s_one[:, 0:1], s_sq,
                 start=(kc == 0), stop=(kc == NKC - 1),
                 wait=[("act", P.c("act"))] + (g2 if kc == 0 else []), inc="pe")
            xt_guard[bi] = ("pe", P.c("pe"))
        pe_sums = P.c("pe")
        S1 = s_rows[:, 0:512]
        S2 = s_rows[:, 512:1024]
        mrow = s_rows[:, 1024:1536]
        arow = s_rows[:, 1536:2048]
        brow = s_rows[:, 2048:2560]
        P.op("vector", "tensor_scalar", mrow, pap1[0:1, :], 1.0 / D, None, ALU.mult,
             wait=("pe", pe_sums), inc="ve")
        P.op("vector", "tensor_scalar", arow, pap2[0:1, :], 1.0 / D, None, ALU.mult,
             inc="ve")
        pp.release(bnk1, "ve", P.c("ve"))
        pp.release(bnk2, "ve", P.c("ve"))
        P.op("vector", "tensor_mul", S1, mrow, mrow, inc="ve")
        P.op("vector", "tensor_sub", arow, arow, S1, inc="ve")
        P.op("scalar", "activation", arow, arow, AF.Ln, bias=s_eps[0:1, 0:1],
             wait=("ve", P.c("ve")))
        P.op("scalar", "activation", arow, arow, AF.Exp, scale=-0.5, inc="act")
        P.op("vector", "scalar_tensor_tensor", brow, mrow, -1.0, arow,
             ALU.mult, ALU.mult, wait=("act", P.c("act")), inc="ve")
        bnka, papa, ga = pp.take(128, 512)
        P.op("tensor", "matmul", papa, s_ones1[:], arow, start=True, stop=True,
             wait=[("ve", P.c("ve"))] + ga, inc="pe")
        bnkb, papbb, gb = pp.take(128, 512)
        P.op("tensor", "matmul", papbb, s_ones1[:], brow, start=True, stop=True,
             wait=gb if gb else None, inc="pe")
        pe_ab = P.c("pe")
        for kc in range(NKC):
            bi = kc % 2
            w = [("cl", cl_all)] + ([xt_guard[bi]] if xt_guard[bi] else [])
            P.op("sync", "dma_start", out=s_xt[bi],
                 in_=x1_r[:, kc, tt * 512:(tt + 1) * 512], wait=w,
                 inc="xl", incn=16)
            P.op("vector", "tensor_mul", s_sq, s_xt[bi], papa,
                 wait=[("xl", P.c("xl")), ("pe", pe_ab)])
            P.op("vector", "tensor_add",
                 s_xn2[:, kc * TPC + tt * 512: kc * TPC + (tt + 1) * 512],
                 s_sq, papbb, inc="ve")
            xt_guard[bi] = ("ve", P.c("ve"))
        pp.release(bnka, "ve", P.c("ve"))
        pp.release(bnkb, "ve", P.c("ve"))
    ve_xn2 = P.c("ve")
    P.op("scalar", "dma_start", out=xn2_r,
         in_=s_xn2.rearrange("p (k t) -> p k t", t=TPC),
         wait=("ve", ve_xn2), inc="st", incn=16)

    wg2 = {}

    def proj_block(wsrc, mb, tag):
        bi = (mb + (0 if tag == "q" else 1)) % 2
        g = [wg2[bi]] if wg2.get(bi) else []
        P.op("gpsimd", "dma_start",
             out=s_w[bi].rearrange("p (k o) -> p k o", o=128),
             in_=wsrc[mb], wait=g, inc="wl", incn=16)
        return P.c("wl"), bi

    def mm_block(bi, wl_m, tt, dst, extra_wait=None):
        bnk, pap, gq = pp.take(128, 512)
        for k in range(NKC):
            w = None
            if k == 0:
                w = [("wl", wl_m), ("ve", ve_xn2)] + gq + (extra_wait or [])
            P.op("tensor", "matmul", pap, s_w[bi][:, k * 128:(k + 1) * 128],
                 s_xn2[:, k * TPC + tt * 512: k * TPC + tt * 512 + 512],
                 start=(k == 0), stop=(k == NKC - 1), wait=w,
                 inc="pe" if k == NKC - 1 else None)
        P.op("vector", "tensor_copy", dst, pap, wait=("pe", P.c("pe")), inc="ve")
        pp.release(bnk, "ve", P.c("ve"))

    def rope_chunk(dst_chunk):
        for a in (0, 64):
            P.op("sync", "dma_start", out=s_qsw[a:a + 32, :],
                 in_=s_qraw[a + 32:a + 64, :], inc="xl", incn=16)
            P.op("sync", "dma_start", out=s_qsw[a + 32:a + 64, :],
                 in_=s_qraw[a:a + 32, :], inc="xl", incn=16)
        xl_sw = P.c("xl")
        P.op("vector", "tensor_mul", s_qraw[:], s_qraw[:], s_cos[:],
             wait=("xl", xl_sw))
        P.op("vector", "tensor_mul", s_qsw[:], s_qsw[:], s_csb[:])
        P.op("vector", "tensor_add", dst_chunk, s_qraw[:], s_qsw[:], inc="ve")

    for mb in range(NKC):
        wl_m, bi = proj_block(wq_t, mb, "q")
        for tt in range(2):
            mm_block(bi, wl_m, tt, s_qraw[:, tt * 512:(tt + 1) * 512])
        wg2[bi] = ("pe", P.c("pe"))
        rope_chunk(s_q[:, mb * TPC:(mb + 1) * TPC])
    for mb in range(4):
        wl_m, bi = proj_block(wk_t, mb, "k")
        for tt in range(2):
            mm_block(bi, wl_m, tt, s_qraw[:, tt * 512:(tt + 1) * 512])
        wg2[bi] = ("pe", P.c("pe"))
        rope_chunk(s_k[:, mb * TPC:(mb + 1) * TPC])
    for mb in range(4):
        wl_m, bi = proj_block(wv_t, mb, "v")
        for tt in range(2):
            mm_block(bi, wl_m, tt, s_v[:, mb * TPC + tt * 512: mb * TPC + tt * 512 + 512])
        wg2[bi] = ("pe", P.c("pe"))
    for kc in range(4):
        for j in range(8):
            i16 = (kc * 8 + j) % 2
            pt, g = take16(i16)
            P.op("tensor", "transpose", pt[:, 0:128],
                 s_v[:, kc * TPC + j * 128: kc * TPC + (j + 1) * 128],
                 s_idb[:], wait=g if g else None, inc="pe")
            P.op("vector", "tensor_copy",
                 s_vtm[:, j * 512 + kc * 128: j * 512 + (kc + 1) * 128],
                 pt[:, 0:128], wait=("pe", P.c("pe")), inc="ve")
            pb16_guard[i16] = ("ve", P.c("ve"))
    ve_prep = P.c("ve")
    if debug:
        P.op("scalar", "dma_start", out=o_dq.rearrange("(k p) t -> p k t", p=128),
             in_=s_q[:].rearrange("p (k t) -> p k t", t=TPC),
             wait=("ve", ve_prep), inc="st", incn=16)
        P.op("scalar", "dma_start", out=o_dk.rearrange("(k p) t -> p k t", p=128),
             in_=s_k[:].rearrange("p (k t) -> p k t", t=TPC),
             wait=("ve", ve_prep), inc="st", incn=16)
    P.op("sync", "dma_start",
         out=ag_kv_in[0:512, :].rearrange("(kc p) t -> p kc t", p=128),
         in_=s_k[:].rearrange("p (kc t) -> p kc t", t=TPC),
         wait=("ve", ve_prep), inc="st", incn=16)
    P.op("sync", "dma_start",
         out=ag_kv_in[512:1536, 0:512].rearrange("(j p) c -> p j c", p=128),
         in_=s_vtm[:].rearrange("p (j c) -> p j c", c=512),
         wait=("ve", ve_prep), inc="st", incn=16)
    st_kv = P.c("st")
    P.wait("gpsimd", "st", st_kv)
    P.sem("cc")

    def cc_kv(eng):
        eng.collective_compute(
            "AllGather", ALU.bypass,
            replica_groups=[[0, 1, 2, 3], [4, 5, 6, 7]],
            ins=[ag_kv_in.opt()], outs=[ag_kv_out.opt()]).then_inc(P.sems["cc"], 1)
    P.raw("gpsimd", cc_kv)
    P.bump("cc", 1)
    cc_done = P.c("cc")

    P.barrier()
    s_vt = arb(0, 128, 32 * 520)
    s_P = arb(8320, 128, 32 * 256)
    s_ytmp = arf(12416, 128, 256)
    s_rz = arf(12672, 1, 256)
    s_fm = arf(12928, 128, 96)
    s_frm = arb(13024, 128, 32 * 256)

    mksel = P.din("mksel", [128, NCH * 8 * 3], F32)
    P.op("sync", "dma_start", out=s_fm, in_=mksel[:], inc="cl", incn=16)
    cl2 = P.c("cl")
    for r in range(LANES):
        for kc in range(4):
            for mi in range(NCH):
                gch = LANES * mi + r
                P.op("gpsimd", "dma_start",
                     out=s_kf[:, kc * 4096 + gch * 256: kc * 4096 + (gch + 1) * 256],
                     in_=ag_kv_out[1536 * r + 128 * kc: 1536 * r + 128 * (kc + 1),
                                   mi * 256:(mi + 1) * 256],
                     wait=("cc", cc_done), inc="hl", incn=16)
    P.op("vector", "memset",
         s_vt.rearrange("p (j b2 u) -> p j b2 u", b2=8, u=65)[:, :, :, 64:65],
         1.0, inc="ve")
    for r in range(LANES):
        for m in range(NCH):
            for jj in range(2):
                j2 = 2 * (LANES * m + r) + jj
                for pair in range(4):
                    dst = s_vt[:, j2 * 520 + pair * 130: j2 * 520 + pair * 130 + 130] \
                        .rearrange("p (e u) -> p e u", u=65)[:, :, 0:64]
                    srcv = ag_kv_out[1536 * r + 512 + 256 * m + 128 * jj:
                                     1536 * r + 512 + 256 * m + 128 * (jj + 1),
                                     pair * 128:(pair + 1) * 128] \
                        .rearrange("p (e u) -> p e u", u=64)
                    P.op("gpsimd", "dma_start", out=dst, in_=srcv,
                         wait=("cc", cc_done), inc="hl", incn=16)
    hl_kv = P.c("hl")
    if debug:
        P.op("scalar", "dma_start", out=o_dkf.rearrange("(a p) t -> p a t", p=128),
             in_=s_kf.rearrange("p (a t) -> p a t", t=4096),
             wait=("hl", hl_kv), inc="st", incn=16)
        P.op("scalar", "dma_start", out=o_dvt.rearrange("(a p) t -> p a t", p=128),
             in_=s_vt.rearrange("p (a t) -> p a t", t=520),
             wait=[("hl", hl_kv), ("ve", P.c("ve"))], inc="st", incn=16)
    for m in range(NCH):
        for f in range(8):
            base = (m * 8 + f) * 3
            fm = s_frm[:, (m * 8 + f) * 256:(m * 8 + f + 1) * 256]
            P.op("vector", "scalar_tensor_tensor", fm, s_cdm0[:],
                 s_fm[:, base + 1:base + 2], s_fm[:, base:base + 1]
                 .broadcast_to([128, 256]),
                 ALU.mult, ALU.add, wait=("cl", cl2))
            P.op("vector", "scalar_tensor_tensor", fm, s_cdm1[:],
                 s_fm[:, base + 2:base + 3], fm, ALU.mult, ALU.add, inc="ve")
    ve_frm = P.c("ve")

    for n in range(H_ATT):
        g4 = n // 8
        e = n % 2
        kcq = n // 2
        kck = g4
        pairv = g4
        for m in range(NCH):
            njk = 8 * m + 8
            bnky, papy, gy = pp.take(65, 256, hold=True)
            for jk in range(njk):
                bnks, paps, gs = pp.take(128, 256)
                P.op("tensor", "matmul", paps,
                     s_kf[64 * e:64 * e + 64, kck * 4096 + jk * 128: kck * 4096 + (jk + 1) * 128],
                     s_q[64 * e:64 * e + 64, kcq * TPC + m * 256: kcq * TPC + (m + 1) * 256],
                     start=True, stop=True,
                     wait=[("hl", hl_kv)] + gs, inc="pe")
                ptile = s_P[:, jk * 256:(jk + 1) * 256]
                P.op("scalar", "activation", ptile, paps, AF.Exp, scale=0.125,
                     wait=("pe", P.c("pe")), inc="act")
                if jk >= 8 * m:
                    f = jk - 8 * m
                    P.op("vector", "tensor_mul", ptile, ptile,
                         s_frm[:, (m * 8 + f) * 256:(m * 8 + f + 1) * 256],
                         wait=[("act", P.c("act")), ("ve", ve_frm)], inc="ve")
                    pp.release(bnks, "ve", P.c("ve"))
                else:
                    pp.release(bnks, "act", P.c("act"))
            act_p = P.c("act")
            ve_p = P.c("ve")
            for jk in range(njk):
                P.op("tensor", "matmul", papy,
                     s_vt[:, jk * 520 + pairv * 130 + e * 65:
                          jk * 520 + pairv * 130 + e * 65 + 65],
                     s_P[:, jk * 256:(jk + 1) * 256],
                     start=(jk == 0), stop=(jk == njk - 1),
                     wait=([("act", act_p), ("ve", ve_p)] + gy) if jk == 0 else None,
                     inc="pe" if jk == njk - 1 else None)
            pe_av = P.c("pe")
            P.op("vector", "tensor_copy", s_ytmp[0:64, :], papy[0:64, :],
                 wait=("pe", pe_av))
            P.op("vector", "reciprocal", s_rz, papy[64:65, :], inc="ve")
            bnkz, papz, gz = pp.take(64, 256)
            P.op("tensor", "matmul", papz, s_ones1[:, 0:64], s_rz,
                 start=True, stop=True, wait=[("ve", P.c("ve"))] + gz, inc="pe")
            P.op("vector", "tensor_mul",
                 s_y[64 * e:64 * e + 64, kcq * TPC + m * 256: kcq * TPC + (m + 1) * 256],
                 s_ytmp[0:64, :], papz, wait=("pe", P.c("pe")), inc="ve")
            pp.release(bnkz, "ve", P.c("ve"))
            pp.guard[bnky] = ("ve", P.c("ve"))
    ve_y = P.c("ve")
    if debug:
        P.op("scalar", "dma_start", out=o_dy.rearrange("(k p) t -> p k t", p=128),
             in_=s_y[:].rearrange("p (k t) -> p k t", t=TPC),
             wait=("ve", ve_y), inc="st", incn=16)

    P.barrier()
    s_wo = [arb(0 + 1024 * iw, 128, NKC * 128) for iw in range(2)]
    s_so = [arb(2048 + 256 * iw, 128, 512) for iw in range(2)]
    s1_r = o_s1.rearrange("(k p) t -> p k t", p=128)
    wog = [None, None]
    sog = [None, None]
    for mb in range(NKC):
        bi = mb % 2
        g = [wog[bi]] if wog[bi] else []
        P.op("gpsimd", "dma_start",
             out=s_wo[bi].rearrange("p (k o) -> p k o", o=128),
             in_=wo_t[mb], wait=g, inc="wl", incn=16)
        wl_m = P.c("wl")
        for tt in range(2):
            bnk, pap, gq = pp.take(128, 512)
            for k in range(NKC):
                w = None
                if k == 0:
                    w = [("wl", wl_m), ("ve", ve_y)] + gq
                P.op("tensor", "matmul", pap, s_wo[bi][:, k * 128:(k + 1) * 128],
                     s_y[:, k * TPC + tt * 512: k * TPC + tt * 512 + 512],
                     start=(k == 0), stop=(k == NKC - 1), wait=w,
                     inc="pe" if k == NKC - 1 else None)
            sg_ = [sog[tt]] if sog[tt] else []
            P.op("vector", "tensor_copy", s_so[tt], pap,
                 wait=[("pe", P.c("pe"))] + sg_, inc="ve")
            P.op("scalar", "dma_start",
                 out=s1_r[:, mb, tt * 512:(tt + 1) * 512], in_=s_so[tt],
                 wait=("ve", P.c("ve")), inc="st", incn=16)
            sog[tt] = ("st", P.c("st"))
            pp.release(bnk, "ve", P.c("ve"))
        wog[bi] = ("pe", P.c("pe"))
    P.wait("gpsimd", "st", P.c("st"))
    return P




def build_E(P=None, pp=None, pb16=None):
    if P is None:
        P = Prog("E")
    P.prefix = "E_"

    i_xn2 = P.din("xn2", [D, TPC], BF16)
    i_x1 = P.din("x1", [D, TPC], F32)
    i_s1 = P.din("s1", [D, TPC], BF16)
    wr_t = P.din("wr_t", [D, E_MOE], F32)
    bias_bc = P.din("bias_bc", [128, E_MOE], F32)
    wg_d = P.din("moe_wg", [E_MOE, 8, 128, NKC, 128], F32)
    wu_d = P.din("moe_wu", [E_MOE, 8, 128, NKC, 128], F32)
    wd_d = P.din("moe_wd", [E_MOE, NKC, 128, 8, 128], F32)
    swg_d = P.din("swg", [8, 128, NKC, 128], F32)
    swu_d = P.din("swu", [8, 128, NKC, 128], F32)
    swd_d = P.din("swd", [NKC, 128, 8, 128], F32)
    ident_f = P.din("ident_f32", [128, 128], F32)

    o_out = P.dout("out_rows", [TPC, D], F32)

    s_xn2 = P.sbuf("s_xn2", [128, NKC * TPC], BF16)
    s_rt = P.sbuf("s_rt", [128, NKC * TPC], F32)
    s_h = P.sbuf("s_h", [128, 8 * TPC], BF16)
    s_wg = [P.sbuf(f"s_wg{i}", [128, NKC * 128], BF16) for i in range(2)]
    s_wu = [P.sbuf(f"s_wu{i}", [128, NKC * 128], BF16) for i in range(2)]
    s_wd = [P.sbuf(f"s_wd{i}", [128, 8 * 128], BF16) for i in range(2)]
    s_wr = P.sbuf("s_wr", [128, NKC * E_MOE], BF16)
    s_bias = P.sbuf("s_bias", [128, E_MOE], F32)
    s_lg = P.sbuf("s_lg", [128, 8 * E_MOE], F32)
    s_wsel = P.sbuf("s_wsel", [128, 8 * E_MOE], F32)
    s_col = P.sbuf("s_col", [128, 8], F32)
    s_tmp = P.sbuf("s_tmp", [128, E_MOE], F32)
    s_diag = P.sbuf("s_diag", [128, 128], F32)
    s_g = P.sbuf("s_g", [128, 512], BF16)
    s_x1 = P.sbuf("s_x1t", [128, 512], F32)
    s_s1 = P.sbuf("s_s1t", [128, 512], BF16)
    s_x2 = P.sbuf("s_x2", [128, NKC * 512], F32)
    s_or = [P.sbuf(f"s_or{i}", [128, D], F32) for i in range(2)]
    s_idf = P.sbuf("s_idf", [128, 128], F32)
    s_one = P.sbuf("s_one", [128, 1], F32)

    if pp is None:
        pp = PsumPool(P, nbanks=6)

    P.op("sync", "dma_start", out=s_idf[:], in_=ident_f[:], inc="cl", incn=16)
    P.op("sync", "dma_start", out=s_bias[:], in_=bias_bc[:], inc="cl", incn=16)
    P.op("gpsimd", "dma_start", out=s_wr[:].rearrange("p (k e) -> p k e", e=E_MOE),
         in_=wr_t.rearrange("(k p) e -> p k e", p=128), inc="cl", incn=16)
    P.op("sync", "dma_start", out=s_xn2[:].rearrange("p (k t) -> p k t", t=TPC),
         in_=i_xn2.rearrange("(k p) t -> p k t", p=128), inc="cl", incn=16)
    cl_all = P.c("cl")
    P.op("vector", "memset", s_one[:], 1.0)
    P.op("vector", "memset", s_rt[:], 0.0, inc="ve")
    ve_init = P.c("ve")

    for tc in range(8):
        bnk, pap, g = pp.take(128, E_MOE)
        for k in range(NKC):
            P.op("tensor", "matmul", pap,
                 s_xn2[:, k * TPC + tc * 128: k * TPC + (tc + 1) * 128],
                 s_wr[:, k * E_MOE:(k + 1) * E_MOE],
                 start=(k == 0), stop=(k == NKC - 1),
                 wait=([("cl", cl_all)] + g) if k == 0 else None,
                 inc="pe" if k == NKC - 1 else None)
        lg = s_lg[:, tc * E_MOE:(tc + 1) * E_MOE]
        P.op("vector", "tensor_add", lg, pap, s_bias[:],
             wait=("pe", P.c("pe")), inc="ve")
        pp.release(bnk, "ve", P.c("ve"))
        m1 = s_col[:, 0:1]
        nm1 = s_col[:, 1:2]
        z = s_col[:, 2:3]
        rz = s_col[:, 3:4]
        m2 = s_col[:, 4:5]
        sm = s_col[:, 5:6]
        rd = s_col[:, 6:7]
        P.op("vector", "reduce_max", m1, lg, AX.X, inc="ve")
        P.op("vector", "tensor_scalar", nm1, m1, -1.0, None, ALU.mult,
             wait=("ve", P.c("ve")), inc="ve")
        P.op("scalar", "activation", lg, lg, AF.Exp, bias=nm1[:, 0:1],
             wait=("ve", P.c("ve")), inc="act")
        P.op("vector", "reduce_sum", z, lg, AX.X, wait=("act", P.c("act")), inc="ve")
        P.op("vector", "reciprocal", rz, z, wait=("ve", P.c("ve")), inc="ve")
        P.op("vector", "tensor_scalar", lg, lg, rz[:, 0:1], None, ALU.mult,
             wait=("ve", P.c("ve")), inc="ve")
        ws = s_wsel[:, tc * E_MOE:(tc + 1) * E_MOE]
        P.op("vector", "reduce_max", m1, lg, AX.X, wait=("ve", P.c("ve")), inc="ve")
        P.op("vector", "tensor_scalar", s_tmp[:], lg, m1[:, 0:1], None, ALU.is_ge,
             wait=("ve", P.c("ve")), inc="ve")
        P.op("vector", "scalar_tensor_tensor", s_tmp[:], s_tmp[:], -2.0, lg,
             ALU.mult, ALU.add, wait=("ve", P.c("ve")), inc="ve")
        P.op("vector", "reduce_max", m2, s_tmp[:], AX.X,
             wait=("ve", P.c("ve")), inc="ve")
        P.op("vector", "tensor_add", sm, m1, m2, wait=("ve", P.c("ve")), inc="ve")
        P.op("vector", "tensor_scalar", sm, sm, 1e-9, None, ALU.add,
             wait=("ve", P.c("ve")), inc="ve")
        P.op("vector", "reciprocal", rd, sm, wait=("ve", P.c("ve")), inc="ve")
        P.op("vector", "tensor_scalar", s_tmp[:], lg, m2[:, 0:1], None, ALU.is_ge,
             wait=("ve", P.c("ve")), inc="ve")
        P.op("vector", "tensor_mul", ws, lg, s_tmp[:], inc="ve")
        P.op("vector", "tensor_scalar", ws, ws, rd[:, 0:1], None, ALU.mult,
             wait=("ve", P.c("ve")), inc="ve")
    ve_router = P.c("ve")

    def expert(eidx):
        routed = eidx >= 0
        for mi in range(8):
            bi = mi % 2
            for wsb, wdr in ((s_wg[bi], wg_d if routed else swg_d),
                             (s_wu[bi], wu_d if routed else swu_d)):
                srcw = (wdr[eidx, mi] if routed else wdr[mi])
                P.op("gpsimd", "dma_start",
                     out=wsb[:].rearrange("p (k i) -> p k i", i=128),
                     in_=srcw,
                     wait=[wgg[bi]] if wgg.get(bi) else None, inc="wl", incn=16)
            wl_m = P.c("wl")
            for tt in range(2):
                bnkg, papg, gg = pp.take(128, 512)
                bnku, papu, gu_ = pp.take(128, 512)
                for k in range(NKC):
                    P.op("tensor", "matmul", papg, s_wg[bi][:, k * 128:(k + 1) * 128],
                         s_xn2[:, k * TPC + tt * 512: k * TPC + tt * 512 + 512],
                         start=(k == 0), stop=(k == NKC - 1),
                         wait=([("wl", wl_m)] + gg) if k == 0 else None,
                         inc="pe" if k == NKC - 1 else None)
                for k in range(NKC):
                    P.op("tensor", "matmul", papu, s_wu[bi][:, k * 128:(k + 1) * 128],
                         s_xn2[:, k * TPC + tt * 512: k * TPC + tt * 512 + 512],
                         start=(k == 0), stop=(k == NKC - 1),
                         wait=gu_ if (k == 0 and gu_) else None,
                         inc="pe" if k == NKC - 1 else None)
                pe_gu = P.c("pe")
                P.op("scalar", "activation", s_g[:], papg, AF.Silu,
                     wait=("pe", pe_gu), inc="act")
                hsl = s_h[:, mi * TPC + tt * 512: mi * TPC + tt * 512 + 512]
                P.op("vector", "tensor_mul", hsl, s_g[:], papu,
                     wait=("act", P.c("act")), inc="ve")
                if routed:
                    bnkb, papb, gb = pp.take(128, 512)
                    for q in range(4):
                        tc = tt * 4 + q
                        P.op("vector", "tensor_scalar", s_diag[:], s_idf[:],
                             s_wsel[:, tc * E_MOE + eidx: tc * E_MOE + eidx + 1],
                             None, ALU.mult, wait=("ve", ve_router) if q == 0 else None,
                             inc="ve")
                        P.op("tensor", "matmul", papb[:, q * 128:(q + 1) * 128],
                             s_onesq[:], s_diag[:], start=True, stop=True,
                             wait=[("ve", P.c("ve"))] + (gb if q == 0 else []),
                             inc="pe" if q == 3 else None)
                    P.op("vector", "tensor_mul", hsl, hsl, papb,
                         wait=("pe", P.c("pe")), inc="ve")
                    pp.release(bnkb, "ve", P.c("ve"))
                pp.release(bnkg, "ve", P.c("ve"))
                pp.release(bnku, "ve", P.c("ve"))
            wgg[bi] = ("pe", P.c("pe"))
        ve_h = P.c("ve")
        for mb in range(NKC):
            bi = mb % 2
            srcd = (wd_d[eidx, mb] if routed else swd_d[mb])
            P.op("gpsimd", "dma_start",
                 out=s_wd[bi][:].rearrange("p (k h) -> p k h", h=128),
                 in_=srcd,
                 wait=[wdg[bi]] if wdg.get(bi) else None, inc="wl", incn=16)
            wl_m = P.c("wl")
            for tt in range(2):
                bnk, pap, g = pp.take(128, 512)
                for ki in range(8):
                    P.op("tensor", "matmul", pap, s_wd[bi][:, ki * 128:(ki + 1) * 128],
                         s_h[:, ki * TPC + tt * 512: ki * TPC + tt * 512 + 512],
                         start=(ki == 0), stop=(ki == 7),
                         wait=([("wl", wl_m), ("ve", ve_h)] + g) if ki == 0 else None,
                         inc="pe" if ki == 7 else None)
                rsl = s_rt[:, mb * TPC + tt * 512: mb * TPC + tt * 512 + 512]
                P.op("vector", "tensor_add", rsl, rsl, pap,
                     wait=[("pe", P.c("pe")), ("ve", ve_init)], inc="ve")
                pp.release(bnk, "ve", P.c("ve"))
            wdg[bi] = ("pe", P.c("pe"))

    s_onesq = P.sbuf("s_onesq", [128, 128], F32)
    P.op("vector", "memset", s_onesq[:], 1.0, inc="ve")
    wgg = {}
    wdg = {}
    expert(-1)
    for e in range(E_MOE):
        expert(e)
    ve_rt = P.c("ve")

    x1_r = i_x1.rearrange("(k p) t -> p k t", p=128)
    s1_r = i_s1.rearrange("(k p) t -> p k t", p=128)
    or_guard = [None, None]
    for tt in range(2):
        for kc in range(NKC):
            P.op("scalar", "activation", s_g[:],
                 s_rt[:, kc * TPC + tt * 512: kc * TPC + tt * 512 + 512], AF.Silu,
                 wait=[("ve", ve_rt), ("xl", P.c("xl"))], inc="act")
            P.op("sync", "dma_start", out=s_x1[:],
                 in_=x1_r[:, kc, tt * 512:(tt + 1) * 512],
                 wait=("ve", P.c("ve")), inc="xl", incn=16)
            P.op("sync", "dma_start", out=s_s1[:],
                 in_=s1_r[:, kc, tt * 512:(tt + 1) * 512], inc="xl", incn=16)
            xl_c = P.c("xl")
            x2sl = s_x2[:, kc * 512:(kc + 1) * 512]
            P.op("vector", "tensor_mul", x2sl, s_s1[:], s_g[:],
                 wait=[("xl", xl_c), ("act", P.c("act"))])
            P.op("vector", "tensor_add", x2sl, x2sl, s_x1[:], inc="ve")
        ve_x2 = P.c("ve")
        for j in range(4):
            bj = j % 2
            og = [or_guard[bj]] if or_guard[bj] else []
            for kc in range(NKC):
                bnk, pap, g = pp.take(128, 128)
                P.op("tensor", "transpose", pap,
                     s_x2[:, kc * 512 + j * 128: kc * 512 + (j + 1) * 128],
                     s_idf[:], wait=[("ve", ve_x2)] + g, inc="pe")
                P.op("vector", "tensor_copy",
                     s_or[bj][:, kc * 128:(kc + 1) * 128], pap,
                     wait=[("pe", P.c("pe"))] + (og if kc == 0 else []), inc="ve")
                pp.release(bnk, "ve", P.c("ve"))
            P.op("scalar", "dma_start",
                 out=o_out[tt * 512 + j * 128: tt * 512 + (j + 1) * 128, :],
                 in_=s_or[bj][:], wait=("ve", P.c("ve")), inc="st", incn=16)
            or_guard[bj] = ("st", P.c("st"))
    P.wait("gpsimd", "st", P.c("st"))
    return P



INTER = {"x_fm", "xn", "z", "xbc", "dt", "s0", "g0", "x1", "s1", "xn2"}


def build_fused():
    P = Prog("full")
    P.fused = True
    P.inter = set(INTER)
    pp = PsumPool(P, nbanks=6)
    pb16 = [P.psum(f"px{i}", [128, 1024], BF16) for i in range(2)]
    for b in (build_A, build_B, build_C, build_D, build_E):
        with P.scope():
            b(P, pp, pb16)
        P.barrier()
    return P



def _run(P, in_maps):
    nc = P.finalize()
    res = run_bass_kernel_spmd(nc, in_maps, core_ids=list(range(NCORES)))
    return res.results


def kernel(**inputs):
    shared, per_core = host_prepare(inputs)
    base = [dict(shared, **pc) for pc in per_core]

    if not os.environ.get("KMULTI"):
        re_ = _run(build_fused(), base)
        out = np.zeros((B, L, D), np.float32)
        for c in range(NCORES):
            b = c // LANES
            rows = np.asarray(re_[c]["out_rows"])
            for m, g in enumerate(core_token_rows(c)):
                out[b, CH * g:CH * (g + 1), :] = rows[m * CH:(m + 1) * CH, :]
        return out

    ra = _run(build_A(), base)
    for c in range(NCORES):
        for k in ("x_fm", "xn", "z", "xbc", "dt"):
            base[c][k] = np.asarray(ra[c][k])
    rb = _run(build_B(), base)
    for c in range(NCORES):
        base[c]["s0"] = np.asarray(rb[c]["s0"])
    rc = _run(build_C(), base)
    for c in range(NCORES):
        base[c]["x1"] = np.asarray(rc[c]["x1"])
    rd = _run(build_D(), base)
    for c in range(NCORES):
        base[c]["s1"] = np.asarray(rd[c]["s1"])
        base[c]["xn2"] = np.asarray(rd[c]["xn2"])
    re_ = _run(build_E(), base)

    out = np.zeros((B, L, D), np.float32)
    for c in range(NCORES):
        b = c // LANES
        rows = np.asarray(re_[c]["out_rows"])
        for m, g in enumerate(core_token_rows(c)):
            out[b, CH * g:CH * (g + 1), :] = rows[m * CH:(m + 1) * CH, :]
    return out
